# version 13 (final)
# baseline (speedup 1.0000x reference)
import numpy as np
import ml_dtypes
from contextlib import ExitStack

import concourse.bass as bass
import concourse.mybir as mybir
from concourse.bass_utils import run_bass_kernel_spmd

F32 = mybir.dt.float32
F16 = mybir.dt.float16
BF16 = mybir.dt.bfloat16
I32 = mybir.dt.int32
AF = mybir.ActivationFunctionType
ALU = mybir.AluOpType
AX = mybir.AxisListType
BF = ml_dtypes.bfloat16

NCORES = 8
D = 4096
DC = D // 128
FF = 11008
FT = FF // 128
TOK = 1024
GRP = 512
NG = 2
NH = 8
E = 64
NHP = 4
NQT = 8
NKB = 16
NKC = 64
SEQ = NCORES * TOK
RSQ = float(512 ** -0.5)
EPS = 1e-5

XFREE = DC * TOK
HFREE = FT * GRP + 1024
H_Q = 0
H_OT = 4096
H_S = 8192
H_P = 24576
H_PT = 32768
H_KSTG = 8192
H_VSTG = 10240
H_CSTG = 12288
WFREE = 16384
GUSLOT = (0, 4096, 8192)
DNSLOT = (0, 5504)




class Ctx:
    def __init__(self, nc, es):
        self.nc = nc
        self.prog = {k: [] for k in ("sync", "scalar", "vector", "tensor", "gpsimd")}
        self.cnt = {}
        self.sem = {}
        for n in ("wg", "ws", "x", "scr", "kv", "sp", "cl", "outd", "dt",
                  "pe", "ac", "ve", "gp"):
            self.sem[n] = es.enter_context(nc.semaphore(n))
            self.cnt[n] = 0

    def emit(self, q, fn):
        self.prog[q].append(fn)

    def wait(self, q, name, val):
        if val is None or val <= 0:
            return
        s = self.sem[name]
        self.emit(q, lambda e, s=s, v=val: e.wait_ge(s, v))

    def waitf(self, q, fence):
        if fence is None:
            return
        if isinstance(fence, list):
            for f in fence:
                self.waitf(q, f)
            return
        self.wait(q, fence[0], fence[1])

    def op(self, q, fn, inc_sem=None, inc_by=1):
        if inc_sem is None:
            self.emit(q, fn)
            return None
        s = self.sem[inc_sem]
        v = self.cnt[inc_sem] + inc_by
        self.cnt[inc_sem] = v
        self.emit(q, lambda e, s=s, fn=fn, k=inc_by: fn(e).then_inc(s, k))
        return v


def build(stop_after="full"):
    nc = bass.Bass("TRN2", target_bir_lowering=False)

    xT = nc.dram_tensor("xT", [D, TOK], F32, kind="ExternalInput").ap()
    bgu = nc.dram_tensor("bgu", [FT, 128, 8192], I32, kind="ExternalInput").ap()
    bd = nc.dram_tensor("bd", [DC, 128, FF], I32, kind="ExternalInput").ap()
    bqk = nc.dram_tensor("bqk", [8, 128, 4096], F32, kind="ExternalInput").ap()
    bvw = nc.dram_tensor("bvw", [4, 128, 4096], F32, kind="ExternalInput").ap()
    bow = nc.dram_tensor("bow", [DC, 128, 512], F32, kind="ExternalInput").ap()
    gs2 = nc.dram_tensor("gs2", [128, FT], F32, kind="ExternalInput").ap()
    us2 = nc.dram_tensor("us2", [128, FT], F32, kind="ExternalInput").ap()
    ds2 = nc.dram_tensor("ds2", [128, DC], F32, kind="ExternalInput").ap()
    qkb2 = nc.dram_tensor("qkb2", [128, 8], F32, kind="ExternalInput").ap()
    vb2 = nc.dram_tensor("vb2", [128, 4], F32, kind="ExternalInput").ap()
    ob2 = nc.dram_tensor("ob2", [128, DC], F32, kind="ExternalInput").ap()
    eyeD = nc.dram_tensor("eyeD", [128, 128], BF16, kind="ExternalInput").ap()
    onesbD = nc.dram_tensor("onesbD", [128, 1], BF16, kind="ExternalInput").ap()
    onesfD = nc.dram_tensor("onesfD", [1, 128], F32, kind="ExternalInput").ap()
    out = nc.dram_tensor("out", [D, TOK], F32, kind="ExternalOutput").ap()

    wsc_gu = nc.dram_tensor("wsc_gu", [2 * FT, 128, 4096], BF16).ap()
    wsc_d = nc.dram_tensor("wsc_d", [2 * DC, 128, FF // 2], BF16).ap()
    xspill = nc.dram_tensor("xspill", [128, XFREE], BF16).ap()
    k_loc = nc.dram_tensor("k_loc", [512, TOK], BF16).ap()
    k_all = nc.dram_tensor("k_all", [NCORES * 512, TOK], BF16).ap()
    v_loc = nc.dram_tensor("v_loc", [128, 4096], BF16).ap()
    rows2 = nc.dram_tensor("rows2", [4, 512], F32).ap()
    v_all = nc.dram_tensor("v_all", [NCORES * 128, 4096], BF16).ap()

    es = ExitStack()
    with es:
        Xh = es.enter_context(nc.sbuf_tensor("X", [128, XFREE], BF16))
        Hh = es.enter_context(nc.sbuf_tensor("H", [128, HFREE], BF16))
        Wh = es.enter_context(nc.sbuf_tensor("W", [128, WFREE], BF16))
        X, H, W = Xh[:], Hh[:], Wh[:]
        Hf32 = Hh.bitcast(F32)[:]
        Hf16 = Hh.bitcast(F16)[:]
        Wf32 = Wh.bitcast(F32)[:]
        gs = es.enter_context(nc.sbuf_tensor("gs", [128, FT], F32))[:]
        us = es.enter_context(nc.sbuf_tensor("us", [128, FT], F32))[:]
        ds = es.enter_context(nc.sbuf_tensor("ds", [128, DC], F32))[:]
        dsh = es.enter_context(nc.sbuf_tensor("dsh", [128, DC], F32))[:]
        qkb = es.enter_context(nc.sbuf_tensor("qkb", [128, 8], F32))[:]
        vbb = es.enter_context(nc.sbuf_tensor("vbb", [128, 4], F32))[:]
        obb = es.enter_context(nc.sbuf_tensor("obb", [128, DC], F32))[:]
        eye = es.enter_context(nc.sbuf_tensor("eye", [128, 128], BF16))[:]
        onesb = es.enter_context(nc.sbuf_tensor("onesb", [128, 1], BF16))[:]
        onesf = es.enter_context(nc.sbuf_tensor("onesf", [1, 128], F32))[:]
        bc0 = es.enter_context(nc.sbuf_tensor("bc0", [128, GRP], F32))[:]
        bc1 = es.enter_context(nc.sbuf_tensor("bc1", [128, GRP], F32))[:]
        sqt = es.enter_context(nc.sbuf_tensor("sqt", [128, 2 * GRP], BF16))[:]
        sgt = es.enter_context(nc.sbuf_tensor("sgt", [128, 2 * GRP], BF16))[:]
        sut = es.enter_context(nc.sbuf_tensor("sut", [128, GRP], BF16))[:]
        ett = es.enter_context(nc.sbuf_tensor("ett", [128, GRP], F32))[:]
        ott = es.enter_context(nc.sbuf_tensor("ott", [128, 2 * GRP], F32))[:]
        rows = es.enter_context(nc.sbuf_tensor("rows", [1, 1536], F32))[:]
        negm = es.enter_context(nc.sbuf_tensor("negm", [128, 1], F32))[:]
        lcol = es.enter_context(nc.sbuf_tensor("lcol", [128, 1], F32))[:]
        invl = es.enter_context(nc.sbuf_tensor("invl", [128, 2], F32))[:]

        B = [es.enter_context(nc.psum_tensor(f"B{i}", [128, 512], F32))[:] for i in range(6)]
        PBF = es.enter_context(nc.psum_tensor("PBF", [128, 1024], BF16))[:]

        ctx = Ctx(nc, es)
        block = es.enter_context(nc.Block())

        r_m = rows[0:1, 0:512]
        r_q = rows[0:1, 512:1024]
        r_t = rows[0:1, 1024:1536]

        def Xg(dc, g):
            return X[:, dc * TOK + g * GRP:(dc * TOK + (g + 1) * GRP)]

        bankstate = [None] * 6
        pbf_state = None
        bc_users = None

        for (dst, src) in ((gs, gs2), (us, us2), (ds, ds2), (qkb, qkb2), (vbb, vb2),
                           (obb, ob2), (eye, eyeD), (onesb, onesbD), (onesf, onesfD)):
            ctx.op("sync", lambda e, d=dst, s=src: e.dma_start(out=d, in_=s), "ws", 16)
        consts_ws = ctx.cnt["ws"]
        ctx.op("gpsimd", lambda e: e.dma_start(
            out=X.rearrange("p (dc t) -> p dc t", dc=DC),
            in_=xT.rearrange("(dc p) t -> p dc t", p=128)), "x", 16)
        xready = ("x", ctx.cnt["x"])
        ctx.wait("tensor", "ws", consts_ws)
        ctx.wait("scalar", "ws", consts_ws)
        ctx.wait("gpsimd", "ws", consts_ws)
        ctx.wait("vector", "ws", consts_ws)
        ctx.op("vector", lambda e: e.tensor_scalar_mul(dsh, ds, 0.5), "ve")

        def ln_group(g, xfence, keep_rows=None, gp_ok=True):
            nonlocal bankstate, bc_users
            ctx.waitf("tensor", xfence)
            ctx.waitf("vector", xfence)
            for b in (4, 5):
                ctx.waitf("tensor", bankstate[b])
            sq_fence = [None, None]
            for dc in range(DC):
                buf = dc % 2
                if sq_fence[buf] is not None:
                    ctx.wait("vector", "pe", sq_fence[buf])
                ctx.op("vector", lambda e, dc=dc, buf=buf: e.tensor_mul(
                    sqt[:, buf * GRP:(buf + 1) * GRP], Xg(dc, g), Xg(dc, g)), "ve")
                vsq = ctx.cnt["ve"]
                ctx.op("tensor", lambda e, dc=dc: e.matmul(
                    B[4][0:1, :], onesb, Xg(dc, g),
                    start=(dc == 0), stop=(dc == DC - 1)), "pe")
                ctx.wait("tensor", "ve", vsq)
                ctx.op("tensor", lambda e, dc=dc, buf=buf: e.matmul(
                    B[5][0:1, :], onesb, sqt[:, buf * GRP:(buf + 1) * GRP],
                    start=(dc == 0), stop=(dc == DC - 1)), "pe")
                sq_fence[buf] = ctx.cnt["pe"]
            pe_stats = ctx.cnt["pe"]
            ctx.wait("scalar", "pe", pe_stats)
            ctx.op("scalar", lambda e: e.mul(r_m, B[4][0:1, :], 1.0 / D), "ac")
            ctx.op("scalar", lambda e: e.mul(r_q, B[5][0:1, :], 1.0 / D), "ac")
            ac_rows = ctx.cnt["ac"]
            ctx.wait("vector", "ac", ac_rows)
            ctx.op("vector", lambda e: e.tensor_mul(r_t, r_m, r_m), "ve")
            ctx.op("vector", lambda e: e.tensor_sub(r_q, r_q, r_t), "ve")
            ctx.op("vector", lambda e: e.tensor_scalar_add(r_q, r_q, EPS), "ve")
            ve_var = ctx.cnt["ve"]
            ctx.wait("scalar", "ve", ve_var)
            ctx.op("scalar", lambda e: e.activation(r_q, r_q, AF.Ln), "ac")
            ctx.op("scalar", lambda e: e.activation(r_t, r_q, AF.Exp, scale=-0.5), "ac")
            ctx.op("scalar", lambda e: e.activation(r_q, r_q, AF.Exp, scale=0.5), "ac")
            if keep_rows is not None:
                g2 = keep_rows
                ctx.wait("scalar", "ac", ctx.cnt["ac"])
                ctx.op("scalar", lambda e, g2=g2: e.dma_start(
                    out=rows2[g2:g2 + 1, :], in_=r_m), "sp", 16)
                ctx.op("scalar", lambda e, g2=g2: e.dma_start(
                    out=rows2[2 + g2:3 + g2, :], in_=r_q), "sp", 16)
            ac_rsd = ctx.cnt["ac"]
            ctx.wait("tensor", "ac", ac_rsd)
            ctx.op("tensor", lambda e: e.matmul(B[4][:, :], onesf, r_t,
                                                start=True, stop=True), "pe")
            ctx.op("tensor", lambda e: e.matmul(B[5][:, :], onesf, r_m,
                                                start=True, stop=True), "pe")
            pe_bc = ctx.cnt["pe"]
            ctx.wait("scalar", "pe", pe_bc)
            ctx.waitf("scalar", bc_users)
            ctx.op("scalar", lambda e: e.copy(bc0, B[4][:, :]), "ac")
            ctx.op("scalar", lambda e: e.copy(bc1, B[5][:, :]), "ac")
            ac_bc = ctx.cnt["ac"]
            bankstate[4] = ("ac", ac_bc)
            bankstate[5] = ("ac", ac_bc)
            ctx.wait("vector", "ac", ac_bc)
            if gp_ok:
                ctx.wait("gpsimd", "ac", ac_bc)
            for dc in range(DC):
                buf = dc % 2
                q = "vector" if (dc % 2 == 0 or not gp_ok) else "gpsimd"
                sm = "ve" if q == "vector" else "gp"
                ctx.op(q, lambda e, dc=dc, buf=buf: e.tensor_sub(
                    sqt[:, buf * GRP:(buf + 1) * GRP], Xg(dc, g), bc1), sm)
                ctx.op(q, lambda e, dc=dc, buf=buf: e.tensor_mul(
                    Xg(dc, g), sqt[:, buf * GRP:(buf + 1) * GRP], bc0), sm)
            out = [("ve", ctx.cnt["ve"])]
            if gp_ok:
                out.append(("gp", ctx.cnt["gp"]))
            return out

        class HLN:

            def __init__(self, g, bs, bq, keep_slot=None, scratch=None):
                self.g, self.bs, self.bq = g, bs, bq
                self.keep_slot = keep_slot
                self.scratch = scratch
                self.started = False
                self.nstat = 0
                self.sq_fence = [None, None]
                self.fence = None

            def stats(self, upto, chunk_fence=None):
                g, bs, bq = self.g, self.bs, self.bq
                if not self.started:
                    ctx.waitf("tensor", bankstate[bs])
                    ctx.waitf("tensor", bankstate[bq])
                    self.started = True
                for dc in range(self.nstat, upto):
                    buf = dc % 2 if self.scratch is None else 0
                    sqv = sqt[:, buf * GRP:(buf + 1) * GRP] \
                        if self.scratch is None else self.scratch
                    if chunk_fence is not None:
                        ctx.wait("vector", chunk_fence[0], chunk_fence[1])
                        ctx.wait("tensor", chunk_fence[0], chunk_fence[1])
                    if self.sq_fence[buf] is not None:
                        ctx.wait("vector", "pe", self.sq_fence[buf])
                    ctx.op("vector", lambda e, dc=dc, g=g, sqv=sqv: e.tensor_mul(
                        sqv, Xg(dc, g), Xg(dc, g)), "ve")
                    vsq = ctx.cnt["ve"]
                    ctx.op("tensor", lambda e, dc=dc, g=g, bs=bs: e.matmul(
                        B[bs][0:1, :], onesb, Xg(dc, g),
                        start=(dc == 0), stop=(dc == DC - 1)), "pe")
                    ctx.wait("tensor", "ve", vsq)
                    ctx.op("tensor", lambda e, dc=dc, bq=bq, sqv=sqv: e.matmul(
                        B[bq][0:1, :], onesb, sqv,
                        start=(dc == 0), stop=(dc == DC - 1)), "pe")
                    self.sq_fence[buf] = ctx.cnt["pe"]
                self.nstat = upto

            def rows_bcast(self):
                bs, bq = self.bs, self.bq
                pe_stats = ctx.cnt["pe"]
                ctx.wait("scalar", "pe", pe_stats)
                ctx.op("scalar", lambda e, bs=bs: e.mul(r_m, B[bs][0:1, :], 1.0 / D), "ac")
                ctx.op("scalar", lambda e, bq=bq: e.mul(r_q, B[bq][0:1, :], 1.0 / D), "ac")
                ac_rows = ctx.cnt["ac"]
                ctx.wait("vector", "ac", ac_rows)
                ctx.op("vector", lambda e: e.tensor_mul(r_t, r_m, r_m), "ve")
                ctx.op("vector", lambda e: e.tensor_sub(r_q, r_q, r_t), "ve")
                ctx.op("vector", lambda e: e.tensor_scalar_add(r_q, r_q, EPS), "ve")
                ve_var = ctx.cnt["ve"]
                ctx.wait("scalar", "ve", ve_var)
                ctx.op("scalar", lambda e: e.activation(r_q, r_q, AF.Ln), "ac")
                ctx.op("scalar", lambda e: e.activation(r_t, r_q, AF.Exp, scale=-0.5), "ac")
                ctx.op("scalar", lambda e: e.activation(r_q, r_q, AF.Exp, scale=0.5), "ac")
                if self.keep_slot is not None:
                    g2 = self.keep_slot
                    ctx.wait("scalar", "ac", ctx.cnt["ac"])
                    ctx.op("scalar", lambda e, g2=g2: e.dma_start(
                        out=rows2[g2:g2 + 1, :], in_=r_m), "sp", 16)
                    ctx.op("scalar", lambda e, g2=g2: e.dma_start(
                        out=rows2[2 + g2:3 + g2, :], in_=r_q), "sp", 16)
                ac_rsd = ctx.cnt["ac"]
                ctx.wait("tensor", "ac", ac_rsd)
                ctx.op("tensor", lambda e, bs=bs: e.matmul(
                    B[bs][:, :], onesf, r_t, start=True, stop=True), "pe")
                ctx.op("tensor", lambda e, bq=bq: e.matmul(
                    B[bq][:, :], onesf, r_m, start=True, stop=True), "pe")
                self.pe_bc = ctx.cnt["pe"]
                self.nnorm = 0

            def norm(self, upto):
                g, bs, bq = self.g, self.bs, self.bq
                if self.nnorm == 0:
                    ctx.wait("vector", "pe", self.pe_bc)
                for dc in range(self.nnorm, upto):
                    buf = dc % 2 if self.scratch is None else 0
                    sqv = sqt[:, buf * GRP:(buf + 1) * GRP] \
                        if self.scratch is None else self.scratch
                    ctx.op("vector", lambda e, dc=dc, g=g, bq=bq, sqv=sqv: e.tensor_sub(
                        sqv, Xg(dc, g), B[bq][:, :]), "ve")
                    ctx.op("vector", lambda e, dc=dc, g=g, bs=bs, sqv=sqv: e.tensor_mul(
                        Xg(dc, g), sqv, B[bs][:, :]), "ve")
                self.nnorm = upto
                if upto == DC:
                    v = ctx.cnt["ve"]
                    bankstate[self.bs] = ("ve", v)
                    bankstate[self.bq] = ("ve", v)
                    self.fence = [("ve", v)]

        def ffn_pass(pass_idx, xfence, warena_fence, tail_jobs=None, g0_hosted=None):
            nonlocal bankstate, bc_users
            guslot_free = [warena_fence] * 3
            dnslot_free = [warena_fence] * 2
            ott_free = [0, 0]
            epi_ve = [[0] * DC, [0] * DC]
            hosted_next = None
            for g in range(NG):
                first = (pass_idx == 0 and g == 0)
                scratch_ready = ctx.cnt["scr"]
                if g == 0 and g0_hosted is not None:
                    xln_f = g0_hosted.fence
                elif hosted_next is not None:
                    xln_f = hosted_next.fence
                    hosted_next = None
                else:
                    xln_f = ln_group(g, xfence if g == 0 else None,
                                     gp_ok=not (pass_idx == 0 and g == 0))
                h_fence = 0
                scr_waited = False
                for ft in range(FT):
                    wv = [None, None]
                    for mat in range(2):
                        slot = (2 * ft + mat) % 3
                        int32_src = pass_idx == 0 and (g == 0 or mat == 1)
                        write_scr = pass_idx == 0 and ((g == 0 and mat == 0) or
                                                       (g == 1 and mat == 1))
                        if int32_src:
                            ctx.waitf("gpsimd", guslot_free[slot])
                            v = ctx.op("gpsimd", lambda e, ft=ft, mat=mat, slot=slot:
                                       e.dma_start(
                                           out=W[:, GUSLOT[slot]:GUSLOT[slot] + 4096],
                                           in_=bgu[ft, :, mat * 4096:(mat + 1) * 4096]),
                                       "wg", 16)
                            wv[mat] = ("wg", v)
                            guslot_free[slot] = []
                            if write_scr:
                                ctx.wait("sync", "wg", v)
                                sv = ctx.op("sync", lambda e, ft=ft, mat=mat, slot=slot:
                                            e.dma_start(
                                                out=wsc_gu[2 * ft + mat],
                                                in_=W[:, GUSLOT[slot]:GUSLOT[slot] + 4096]),
                                            "scr", 16)
                                guslot_free[slot] = [("scr", sv)]
                        else:
                            ctx.waitf("sync", guslot_free[slot])
                            if not scr_waited and pass_idx == 0 and g == 1:
                                ctx.wait("sync", "scr", scratch_ready)
                                scr_waited = True
                            v = ctx.op("sync", lambda e, ft=ft, mat=mat, slot=slot:
                                       e.dma_start(
                                           out=W[:, GUSLOT[slot]:GUSLOT[slot] + 4096],
                                           in_=wsc_gu[2 * ft + mat]), "ws", 16)
                            wv[mat] = ("ws", v)
                            guslot_free[slot] = []
                    gb, ub = (0, 1) if ft % 2 == 0 else (2, 3)
                    sgi, sui = (2 * ft) % 3, (2 * ft + 1) % 3
                    ctx.waitf("tensor", bankstate[gb])
                    ctx.waitf("tensor", wv[0])
                    if ft == 0:
                        ctx.waitf("tensor", xln_f)
                    for dc in range(DC):
                        ctx.op("tensor", lambda e, dc=dc, slot=sgi, gb=gb, g=g: e.matmul(
                            B[gb][:, :],
                            W[:, GUSLOT[slot] + dc * 128:GUSLOT[slot] + (dc + 1) * 128],
                            Xg(dc, g), start=(dc == 0), stop=(dc == DC - 1)), "pe")
                    pe_g = ctx.cnt["pe"]
                    guslot_free[sgi] = (guslot_free[sgi] if isinstance(guslot_free[sgi], list)
                                        else []) + [("pe", pe_g)]
                    ctx.waitf("tensor", bankstate[ub])
                    ctx.waitf("tensor", wv[1])
                    for dc in range(DC):
                        ctx.op("tensor", lambda e, dc=dc, slot=sui, ub=ub, g=g: e.matmul(
                            B[ub][:, :],
                            W[:, GUSLOT[slot] + dc * 128:GUSLOT[slot] + (dc + 1) * 128],
                            Xg(dc, g), start=(dc == 0), stop=(dc == DC - 1)), "pe")
                    pe_u = ctx.cnt["pe"]
                    guslot_free[sui] = (guslot_free[sui] if isinstance(guslot_free[sui], list)
                                        else []) + [("pe", pe_u)]
                    eb = ft % 2
                    ctx.wait("scalar", "pe", pe_g)
                    ac_g = ctx.op("scalar", lambda e, ft=ft, eb=eb, gb=gb: e.activation(
                        sgt[:, eb * GRP:(eb + 1) * GRP], B[gb][:, :], AF.Silu,
                        scale=gs[:, ft:ft + 1]), "ac")
                    bankstate[gb] = ("ac", ac_g)
                    ctx.wait("vector", "pe", pe_u)
                    ctx.op("vector", lambda e, ft=ft, eb=eb, ub=ub: e.tensor_scalar(
                        sut, B[ub][:, :], us[:, ft:ft + 1],
                        None, ALU.mult), "ve")
                    bankstate[ub] = ("ve", ctx.cnt["ve"])
                    ctx.wait("vector", "ac", ac_g)
                    ctx.op("vector", lambda e, ft=ft, eb=eb: e.tensor_mul(
                        H[:, ft * GRP:(ft + 1) * GRP], sgt[:, eb * GRP:(eb + 1) * GRP],
                        sut), "ve")
                    h_fence = ctx.cnt["ve"]
                gu_end = ("pe", ctx.cnt["pe"])
                dn_extra = [f for s in guslot_free if isinstance(s, list) for f in s]
                dnslot_free = [dnslot_free[0] if g == 0 and pass_idx == 0 else None,
                               None]
                dnslot_free = [dn_extra + [gu_end], dn_extra + [gu_end]]
                ctx.waitf("tensor", bankstate[4])
                ctx.waitf("tensor", bankstate[5])
                ctx.op("tensor", lambda e: e.matmul(B[4][:, :], onesf, r_q,
                                                    start=True, stop=True), "pe")
                ctx.op("tensor", lambda e: e.matmul(B[5][:, :], onesf, r_m,
                                                    start=True, stop=True), "pe")
                pe_bc = ctx.cnt["pe"]
                ctx.wait("scalar", "pe", pe_bc)
                ctx.waitf("scalar", bc_users)
                ctx.op("scalar", lambda e: e.copy(bc0, B[4][:, :]), "ac")
                ac_sdb = ctx.op("scalar", lambda e: e.copy(bc1, B[5][:, :]), "ac")
                bankstate[4] = ("ac", ac_sdb)
                bankstate[5] = ("ac", ac_sdb)
                ctx.wait("vector", "ac", ac_sdb)
                ctx.waitf("vector", xln_f)
                jobs = []
                if g == 0:
                    hosted_next = HLN(1, 4, 5)
                    hl = hosted_next
                    jobs += [(14, lambda hl=hl: hl.stats(16)),
                             (16, lambda hl=hl: (hl.stats(DC), hl.rows_bcast()))]
                    jobs += [(18 + i, lambda hl=hl, i=i: hl.norm(4 * (i + 1)))
                             for i in range(8)]
                if g == NG - 1 and tail_jobs is not None:
                    jobs += tail_jobs(epi_ve[g])
                jobs.sort(key=lambda t: t[0])
                for j in range(DC):
                    for (jj, fn) in jobs:
                        if jj == j:
                            fn()
                    wvh = [None, None]
                    for hh in range(2):
                        slot = hh
                        if pass_idx == 0:
                            ctx.waitf("gpsimd", dnslot_free[slot])
                            v = ctx.op("gpsimd", lambda e, j=j, hh=hh, slot=slot:
                                       e.dma_start(
                                           out=W[:, DNSLOT[slot]:DNSLOT[slot] + 5504],
                                           in_=bd[j, :, hh * 5504:(hh + 1) * 5504]),
                                       "wg", 16)
                            wvh[hh] = ("wg", v)
                            dnslot_free[slot] = []
                            if g == 1:
                                ctx.wait("sync", "wg", v)
                                sv = ctx.op("sync", lambda e, j=j, hh=hh, slot=slot:
                                            e.dma_start(
                                                out=wsc_d[2 * j + hh],
                                                in_=W[:, DNSLOT[slot]:DNSLOT[slot] + 5504]),
                                            "scr", 16)
                                dnslot_free[slot] = [("scr", sv)]
                        else:
                            ctx.waitf("gpsimd", dnslot_free[slot])
                            if j == 0 and hh == 0:
                                ctx.wait("gpsimd", "scr", scratch_ready)
                            v = ctx.op("gpsimd", lambda e, j=j, hh=hh, slot=slot:
                                       e.dma_start(
                                           out=W[:, DNSLOT[slot]:DNSLOT[slot] + 5504],
                                           in_=wsc_d[2 * j + hh]), "wg", 16)
                            wvh[hh] = ("wg", v)
                            dnslot_free[slot] = []
                    db = j % 2
                    ctx.waitf("tensor", bankstate[db])
                    if j == 0:
                        ctx.wait("tensor", "ve", h_fence)
                    for fc in range(FT):
                        hh = fc // 43
                        if fc == 0:
                            ctx.waitf("tensor", wvh[0])
                        elif fc == 43:
                            ctx.waitf("tensor", wvh[1])
                        ctx.op("tensor", lambda e, fc=fc, hh=hh, db=db: e.matmul(
                            B[db][:, :],
                            W[:, DNSLOT[hh] + (fc - hh * 43) * 128:
                                 DNSLOT[hh] + (fc - hh * 43 + 1) * 128],
                            H[:, fc * GRP:(fc + 1) * GRP],
                            start=(fc == 0), stop=(fc == FT - 1)), "pe")
                        if fc == 42:
                            dnslot_free[0] = (dnslot_free[0] if isinstance(dnslot_free[0], list)
                                              else []) + [("pe", ctx.cnt["pe"])]
                    pe_d = ctx.cnt["pe"]
                    dnslot_free[1] = (dnslot_free[1] if isinstance(dnslot_free[1], list)
                                      else []) + [("pe", pe_d)]
                    eb = j % 2
                    ctx.wait("vector", "pe", pe_d)
                    ctx.op("vector", lambda e, j=j, eb=eb, db=db: e.scalar_tensor_tensor(
                        ett, B[db][:, :], dsh[:, j:j + 1],
                        bc1, ALU.mult, ALU.add), "ve")
                    bankstate[db] = ("ve", ctx.cnt["ve"])
                    ctx.op("vector", lambda e, j=j, eb=eb, g=g: e.tensor_mul(
                        sgt[:, eb * GRP:(eb + 1) * GRP], Xg(j, g), bc0), "ve")
                    if pass_idx == 0:
                        ctx.op("vector", lambda e, j=j, eb=eb, g=g: e.tensor_add(
                            Xg(j, g), ett,
                            sgt[:, eb * GRP:(eb + 1) * GRP]), "ve")
                        epi_ve[g][j] = ctx.cnt["ve"]
                    else:
                        ctx.wait("vector", "outd", ott_free[eb])
                        vout = ctx.op("vector", lambda e, j=j, eb=eb: e.tensor_add(
                            ott[:, eb * GRP:(eb + 1) * GRP],
                            ett,
                            sgt[:, eb * GRP:(eb + 1) * GRP]), "ve")
                        ctx.wait("scalar", "ve", vout)
                        ctx.op("scalar", lambda e, j=j, eb=eb, g=g: e.dma_start(
                            out=out.rearrange("(j p) t -> j p t", p=128)[j]
                                [:, g * GRP:(g + 1) * GRP],
                            in_=ott[:, eb * GRP:(eb + 1) * GRP]), "outd", 16)
                        ott_free[eb] = ctx.cnt["outd"]
                for (jj, fn) in jobs:
                    if jj >= DC:
                        fn()
                dn_end = [f for s in dnslot_free if isinstance(s, list) for f in s]
                guslot_free = [dn_end, dn_end, dn_end]
                bc_users = ("ve", ctx.cnt["ve"])
            return ("ve", ctx.cnt["ve"])

        def dump_X(fence):
            ctx.waitf("scalar", fence)
            slot_out = [0, 0]
            for dc in range(DC):
                eb = dc % 2
                ctx.wait("scalar", "outd", ctx.cnt["outd"])
                v1 = ctx.op("scalar", lambda e, dc=dc, eb=eb: e.copy(
                    ott[:, eb * GRP:(eb + 1) * GRP], X[:, dc * TOK:dc * TOK + GRP]), "ac")
                v2 = ctx.op("scalar", lambda e, dc=dc, eb=eb: e.copy(
                    ett,
                    X[:, dc * TOK + GRP:dc * TOK + TOK]), "ac")
                ctx.wait("sync", "ac", v2)
                ctx.op("sync", lambda e, dc=dc, eb=eb: e.dma_start(
                    out=out.rearrange("(j p) t -> j p t", p=128)[dc][:, 0:GRP],
                    in_=ott[:, eb * GRP:(eb + 1) * GRP]), "outd", 16)
                ctx.op("sync", lambda e, dc=dc, eb=eb: e.dma_start(
                    out=out.rearrange("(j p) t -> j p t", p=128)[dc][:, GRP:TOK],
                    in_=ett), "outd", 16)
                slot_out[eb] = ctx.cnt["outd"]

        def attention(xln2_f):
            nonlocal bankstate, pbf_state, bc_users
            ffn1_pe = ctx.cnt["pe"]
            ctx.waitf("scalar", xln2_f)
            sp_v = ctx.op("scalar", lambda e: e.dma_start(out=xspill, in_=X), "sp", 16)

            kstg = [H[:, H_KSTG + i * TOK:H_KSTG + (i + 1) * TOK] for i in range(2)]
            vstg = [H[:, H_VSTG + i * TOK:H_VSTG + (i + 1) * TOK] for i in range(2)]
            cstg = H[:, H_CSTG:H_CSTG + 4096]
            Q = H[:, H_Q:H_Q + 4096]
            OT = H[:, H_OT:H_OT + 4096]
            Sb = [Hf16[:, H_S:H_S + 8192], Hf16[:, H_S + 8192:H_S + 16384]]
            Pp = [H[:, H_P:H_P + 8192], H[:, H_P + 8192:H_P + 16384]]
            PTb = [W[:, 0:8192], W[:, 8192:16384]]

            wslab_b = [W[:, 8192:12288], W[:, 12288:16384]]
            ffn_done = ("pe", ffn1_pe)
            slab_prev = ffn_done
            kstg_free = [ffn_done, ffn_done]
            vstg_free = [ffn_done, ffn_done]
            qkv_pe = 0
            cstg_ve = 0
            kv_last = 0
            first_cstg = True
            qkv_pe_hist = [0, 0]
            slab_v = [0] * 12
            for ht in range(12):
                sl2 = (ht + 1) % 2
                if qkv_pe_hist[sl2]:
                    ctx.wait("gpsimd", "pe", qkv_pe_hist[sl2])
                elif sl2 == 0:
                    ctx.waitf("gpsimd", ffn_done)
                srcw = bqk[ht] if ht < 8 else bvw[ht - 8]
                slab_v[ht] = ctx.op("gpsimd", lambda e, srcw=srcw, sl2=sl2: e.dma_start(
                    out=wslab_b[sl2], in_=srcw), "wg", 16)
                for th in range(2):
                    qb = 2 * (ht % 2) + th
                    ctx.waitf("tensor", bankstate[qb])
                    ctx.wait("tensor", "wg", slab_v[ht])
                    if ht == 0 and th == 0:
                        ctx.waitf("tensor", xln2_f)
                    for dc in range(DC):
                        ctx.op("tensor", lambda e, dc=dc, th=th, qb=qb, sl2=sl2: e.matmul(
                            B[qb][:, :], wslab_b[sl2][:, dc * 128:(dc + 1) * 128],
                            Xg(dc, th), start=(dc == 0), stop=(dc == DC - 1)), "pe")
                    pe_v = ctx.cnt["pe"]
                    qkv_pe = pe_v
                    qkv_pe_hist[sl2] = pe_v
                    ctx.wait("scalar", "pe", pe_v)
                    if ht < 8:
                        hp, qk = ht // 2, ht % 2
                        if qk == 0:
                            av = ctx.op("scalar", lambda e, hp=hp, th=th, qb=qb:
                                        e.activation(
                                            Q[:, hp * TOK + th * GRP:
                                              hp * TOK + (th + 1) * GRP],
                                            B[qb][:, :], AF.Identity,
                                            bias=qkb[:, 2 * hp:2 * hp + 1]), "ac")
                        else:
                            kb_i = hp % 2
                            ctx.waitf("scalar", kstg_free[kb_i])
                            av = ctx.op("scalar", lambda e, th=th, qb=qb, kb_i=kb_i, hp=hp:
                                        e.activation(
                                            kstg[kb_i][:, th * GRP:(th + 1) * GRP],
                                            B[qb][:, :], AF.Identity,
                                            bias=qkb[:, 2 * hp + 1:2 * hp + 2]), "ac")
                    else:
                        vt = ht - 8
                        vb_i = vt % 2
                        ctx.waitf("scalar", vstg_free[vb_i])
                        av = ctx.op("scalar", lambda e, th=th, qb=qb, vt=vt, vb_i=vb_i:
                                    e.activation(
                                        vstg[vb_i][:, th * GRP:(th + 1) * GRP],
                                        B[qb][:, :], AF.Identity,
                                        bias=vbb[:, vt:vt + 1]), "ac")
                    bankstate[qb] = ("ac", av)
                if ht < 8 and ht % 2 == 1:
                    hp = ht // 2
                    kb_i = hp % 2
                    acv = ctx.cnt["ac"]
                    ctx.wait("sync", "ac", acv)
                    wv = ctx.op("sync", lambda e, hp=hp, kb_i=kb_i: e.dma_start(
                        out=k_loc.rearrange("(hp p) t -> hp p t", p=128)[hp],
                        in_=kstg[kb_i]), "kv", 16)
                    kstg_free[kb_i] = ("kv", wv)
                    kv_last = wv
                    if ht == 7:
                        ctx.wait("gpsimd", "kv", kv_last)
                        ctx.op("gpsimd", lambda e: e.collective_compute(
                            "AllGather", ALU.bypass,
                            replica_groups=[list(range(NCORES))],
                            ins=[k_loc], outs=[k_all]), "cl")
                if ht >= 8:
                    vt = ht - 8
                    vb_i = vt % 2
                    acv = ctx.cnt["ac"]
                    ctx.wait("tensor", "ac", acv)
                    if first_cstg:
                        ctx.waitf("vector", ffn_done)
                        first_cstg = False
                    for tc in range(8):
                        ctx.waitf("tensor", pbf_state)
                        ctx.op("tensor", lambda e, tc=tc, vb_i=vb_i: e.transpose(
                            PBF[:, tc * 128:(tc + 1) * 128],
                            vstg[vb_i][:, tc * 128:(tc + 1) * 128], eye), "pe")
                        tv = ctx.cnt["pe"]
                        ctx.wait("vector", "pe", tv)
                        ctx.op("vector", lambda e, vt=vt, tc=tc: e.tensor_copy(
                            cstg[:, ((2 * vt) * 8 + tc) * 64:
                                 ((2 * vt) * 8 + tc) * 64 + 64],
                            PBF[:, tc * 128:tc * 128 + 64]), "ve")
                        ctx.op("vector", lambda e, vt=vt, tc=tc: e.tensor_copy(
                            cstg[:, ((2 * vt + 1) * 8 + tc) * 64:
                                 ((2 * vt + 1) * 8 + tc) * 64 + 64],
                            PBF[:, tc * 128 + 64:(tc + 1) * 128]), "ve")
                        pbf_state = ("ve", ctx.cnt["ve"])
                    vstg_free[vb_i] = ("pe", ctx.cnt["pe"])
                    cstg_ve = ctx.cnt["ve"]
            ctx.wait("sync", "ve", cstg_ve)
            kv_last = ctx.op("sync", lambda e: e.dma_start(out=v_loc, in_=cstg), "kv", 16)
            ctx.wait("gpsimd", "kv", kv_last)
            ctx.op("gpsimd", lambda e: e.collective_compute(
                "AllGather", ALU.bypass, replica_groups=[list(range(NCORES))],
                ins=[v_loc], outs=[v_all]), "cl")
            cl_done = ctx.cnt["cl"]
            ctx.wait("gpsimd", "cl", cl_done - 1)
            ctx.wait("gpsimd", "sp", sp_v)
            ctx.wait("gpsimd", "pe", qkv_pe)
            kload = [None] * 4
            for hp in range(NHP):
                v = ctx.op("gpsimd", lambda e, hp=hp: e.dma_start(
                    out=X[:, hp * 8192:(hp + 1) * 8192].rearrange(
                        "p (r t) -> p r t", r=NCORES),
                    in_=k_all.rearrange("(r b p) t -> b p r t", r=NCORES, b=NHP)[hp]),
                    "kv", 16)
                kload[hp] = ("kv", v)
            vbuf = H[:, 40960:45056]
            vbuf_free = [("pe", qkv_pe)]
            vbuf_loaded = None

            def load_vbuf(h):
                nonlocal vbuf_loaded
                ctx.waitf("gpsimd", vbuf_free)
                ctx.wait("gpsimd", "cl", cl_done)
                v = ctx.op("gpsimd", lambda e, h=h: e.dma_start(
                    out=vbuf.rearrange("p (r f) -> p r f", r=NCORES),
                    in_=v_all.rearrange("(r p) f -> p r f", p=128)
                        [:, :, h * 512:(h + 1) * 512]), "kv", 16)
                vbuf_loaded = ("kv", v)

            load_vbuf(0)
            sc_banks = [None, None, None]
            pv_bank = None
            pt_free = [None, None]
            ot_slot = [None, None]
            ex = [0] * 65
            iv = [0] * 65
            dtv = [0] * 65
            q_ready = ("ac", ctx.cnt["ac"])

            def emit_scores(i):
                h, qt = divmod(i, NQT)
                hp, hh = h // 2, h % 2
                sb = i % 2
                if qt == 0:
                    ctx.waitf("tensor", kload[hp])
                    ctx.waitf("tensor", q_ready)
                for kb in range(NKB):
                    bi = kb % 3
                    ctx.waitf("tensor", sc_banks[bi])
                    ctx.op("tensor", lambda e, hp=hp, hh=hh, qt=qt, kb=kb, bi=bi:
                           e.matmul(
                               B[bi][:, :],
                               Q[hh * 64:(hh + 1) * 64,
                                 hp * TOK + qt * 128:hp * TOK + (qt + 1) * 128],
                               X[hh * 64:(hh + 1) * 64,
                                 hp * 8192 + kb * 512:hp * 8192 + (kb + 1) * 512],
                               start=True, stop=True), "pe")
                    pe_s = ctx.cnt["pe"]
                    if kb % 16 in (0, 3, 6, 9, 12, 15):
                        ctx.wait("scalar", "pe", pe_s)
                        if kb == 0 and i >= 2:
                            ctx.wait("scalar", "ac", ex[i - 2])
                        v = ctx.op("scalar", lambda e, kb=kb, bi=bi, sb=sb: e.copy(
                            Sb[sb][:, kb * 512:(kb + 1) * 512], B[bi][:, :]), "ac")
                        sc_banks[bi] = ("ac", v)
                    else:
                        ctx.wait("vector", "pe", pe_s)
                        if kb == 1 and i >= 2:
                            ctx.wait("vector", "ac", ex[i - 2])
                        v = ctx.op("vector", lambda e, kb=kb, bi=bi, sb=sb:
                                   e.tensor_copy(
                                       Sb[sb][:, kb * 512:(kb + 1) * 512],
                                       B[bi][:, :]), "ve")
                        sc_banks[bi] = ("ve", v)

            def emit_softmax(i):
                sb = i % 2
                ac_evac = ctx.cnt["ac"]
                ctx.wait("vector", "ac", ac_evac)
                if i >= 1:
                    ctx.wait("vector", "ac", ex[i - 1])
                mx = ctx.op("vector", lambda e, sb=sb: e.reduce_max(
                    negm, Sb[sb].rearrange("p (a s) -> p a s", s=8)[:, :, 0:1],
                    axis=AX.XY, negate=True), "ve")
                ctx.wait("scalar", "ve", mx)
                if i >= 2:
                    ctx.wait("scalar", "dt", dtv[i - 2])
                if i >= 1:
                    ctx.wait("scalar", "ve", iv[i - 1])
                ex[i] = ctx.op("scalar", lambda e, sb=sb: e.activation(
                    Pp[sb], Sb[sb], AF.Exp, bias=negm, accum_out=lcol), "ac")

            def emit_transpose(i):
                sb = i % 2
                ctx.wait("sync", "ac", ex[i])
                ctx.waitf("sync", pt_free[sb])
                dtv[i] = ctx.op("sync", lambda e, sb=sb: e.dma_start_transpose(
                    out=PTb[sb].rearrange("p (c q) -> p c q", c=NKC), in_=Pp[sb]),
                    "dt", 16)

            def emit_recip(i):
                sb = i % 2
                ctx.wait("vector", "ac", ex[i])
                iv[i] = ctx.op("vector", lambda e, sb=sb: e.reciprocal(
                    invl[:, sb:sb + 1], lcol), "ve")

            def emit_pv(i):
                nonlocal pv_bank, pt_free, pbf_state, vbuf_free
                h, qt = divmod(i, NQT)
                hp, hh = h // 2, h % 2
                sb = i % 2
                ctx.wait("tensor", "dt", dtv[i])
                ctx.waitf("tensor", pv_bank)
                ctx.waitf("tensor", vbuf_loaded)
                for kc in range(NKC):
                    ctx.op("tensor", lambda e, kc=kc, sb=sb: e.matmul(
                        B[3][:, 0:64], PTb[sb][:, kc * 128:(kc + 1) * 128],
                        vbuf[:, kc * 64:(kc + 1) * 64],
                        start=(kc == 0), stop=(kc == NKC - 1)), "pe")
                pe_pv = ctx.cnt["pe"]
                pt_free[sb] = ("pe", pe_pv)
                osl = i % 2
                ctx.wait("vector", "pe", pe_pv)
                ctx.waitf("vector", ot_slot[osl])
                ov = ctx.op("vector", lambda e, osl=osl, sb=sb: e.tensor_scalar(
                    sgt[:, osl * 64:osl * 64 + 64], B[3][:, 0:64],
                    invl[:, sb:sb + 1], RSQ, ALU.mult, ALU.mult), "ve")
                pv_bank = ("ve", ov)
                ctx.wait("tensor", "ve", ov)
                ctx.waitf("tensor", pbf_state)
                tv = ctx.op("tensor", lambda e, qt=qt, hh=hh, osl=osl: e.transpose(
                    PBF[hh * 64:(hh + 1) * 64, (qt % 8) * 128:(qt % 8 + 1) * 128],
                    sgt[:, osl * 64:osl * 64 + 64], eye), "pe")
                ot_slot[osl] = ("pe", tv)
                ctx.wait("vector", "pe", tv)
                cv = ctx.op("vector", lambda e, qt=qt, hh=hh, hp=hp: e.tensor_copy(
                    OT[hh * 64:(hh + 1) * 64,
                       hp * TOK + qt * 128:hp * TOK + (qt + 1) * 128],
                    PBF[hh * 64:(hh + 1) * 64, (qt % 8) * 128:(qt % 8 + 1) * 128]),
                    "ve")
                pbf_state = ("ve", cv)
                if qt == NQT - 1 and h + 1 < NH:
                    vbuf_free = [("pe", pe_pv)]
                    load_vbuf(h + 1)

            for i in range(NH * NQT + 2):
                if i < NH * NQT:
                    emit_scores(i)
                    emit_softmax(i)
                if i >= 2:
                    emit_pv(i - 2)
                if i < NH * NQT:
                    emit_transpose(i)
                    emit_recip(i)
            attn_pe_end = ctx.cnt["pe"]
            bo_b = [W[:, 4096:4608], W[:, 4608:5120]]
            xst = [W[:, 5120:5632], W[:, 5632:6144]]
            arena_fence = ("pe", attn_pe_end)
            bo_free = [[arena_fence], [arena_fence]]
            xst_free = [[arena_fence], [arena_fence]]
            ctx.wait("sync", "sp", ctx.cnt["sp"])
            hl3 = HLN(0, 4, 5, scratch=sut)
            for th in range(2):
                rlv = ctx.op("sync", lambda e, th=th: e.dma_start(
                    out=r_m, in_=rows2[th:th + 1, :]), "ws", 16)
                rlv = ctx.op("sync", lambda e, th=th: e.dma_start(
                    out=r_t, in_=rows2[2 + th:3 + th, :]), "ws", 16)
                ctx.waitf("tensor", bankstate[4])
                ctx.waitf("tensor", bankstate[5])
                ctx.wait("tensor", "ws", rlv)
                ctx.op("tensor", lambda e: e.matmul(
                    B[4][:, :], onesf, r_t, start=True, stop=True), "pe")
                ctx.op("tensor", lambda e: e.matmul(
                    B[5][:, :], onesf, r_m, start=True, stop=True), "pe")
                pe_bc = ctx.cnt["pe"]
                ctx.wait("scalar", "pe", pe_bc)
                ctx.waitf("scalar", bc_users)
                ctx.op("scalar", lambda e: e.copy(bc0, B[4][:, :]), "ac")
                ac_bc = ctx.op("scalar", lambda e: e.copy(bc1, B[5][:, :]), "ac")
                bankstate[4] = ("ac", ac_bc)
                bankstate[5] = ("ac", ac_bc)
                ctx.wait("vector", "ac", ac_bc)
                opjobs = []
                if th == 1:
                    opjobs = [(4, lambda: hl3.stats(16)),
                              (8, lambda: (hl3.stats(DC), hl3.rows_bcast()))]
                    opjobs += [(10 + 2 * i, lambda i=i: hl3.norm(4 * (i + 1)))
                               for i in range(8)]
                for jt in range(DC):
                    for (jj, fn) in opjobs:
                        if jj == jt:
                            fn()
                    sl = jt % 2
                    ctx.waitf("gpsimd", bo_free[sl])
                    cb = ctx.op("gpsimd", lambda e, jt=jt, sl=sl: e.dma_start(
                        out=bo_b[sl], in_=bow[jt]), "wg", 16)
                    ctx.waitf("sync", xst_free[sl])
                    xv = ctx.op("sync", lambda e, jt=jt, th=th, sl=sl: e.dma_start(
                        out=xst[sl],
                        in_=xspill[:, jt * TOK + th * GRP:jt * TOK + (th + 1) * GRP]),
                        "ws", 16)
                    ob_i = jt % 2
                    ctx.waitf("tensor", bankstate[ob_i])
                    ctx.wait("tensor", "wg", cb)
                    for hc in range(4):
                        ctx.op("tensor", lambda e, hc=hc, th=th, sl=sl, ob_i=ob_i:
                               e.matmul(
                                   B[ob_i][:, :], bo_b[sl][:, hc * 128:(hc + 1) * 128],
                                   OT[:, hc * TOK + th * GRP:hc * TOK + (th + 1) * GRP],
                                   start=(hc == 0), stop=(hc == 3)), "pe")
                    pe_o = ctx.cnt["pe"]
                    bo_free[sl] = [("pe", pe_o)]
                    eb = jt % 2
                    ctx.wait("vector", "pe", pe_o)
                    ctx.wait("vector", "ws", xv)
                    ctx.op("vector", lambda e, jt=jt, eb=eb, ob_i=ob_i:
                           e.scalar_tensor_tensor(
                               ett, B[ob_i][:, :],
                               obb[:, jt:jt + 1], bc1, ALU.add, ALU.add), "ve")
                    bankstate[ob_i] = ("ve", ctx.cnt["ve"])
                    ctx.op("vector", lambda e, sl=sl, eb=eb: e.tensor_mul(
                        sqt[:, eb * GRP:(eb + 1) * GRP], xst[sl], bc0), "ve")
                    xw = ctx.op("vector", lambda e, jt=jt, th=th, eb=eb: e.tensor_add(
                        Xg(jt, th), ett,
                        sqt[:, eb * GRP:(eb + 1) * GRP]), "ve")
                    xst_free[sl] = [("ve", xw)]
                bc_users = ("ve", ctx.cnt["ve"])
            wa = [("pe", ctx.cnt["pe"]), ("ve", ctx.cnt["ve"])]
            return ("ve", ctx.cnt["ve"]), wa, hl3

        if stop_after == "ffn1":
            xfence = ffn_pass(0, xready, None)
            dump_X(xfence)
        else:
            ln2a = HLN(0, 4, 5, keep_slot=0)
            ln2b = HLN(1, 2, 3, keep_slot=1, scratch=sut)

            def ln2_jobs(epi):
                jobs = [(15, lambda: ln2a.stats(16)),
                        (17, lambda: (ln2a.stats(DC), ln2a.rows_bcast()))]
                jobs += [(18 + i, lambda i=i: ln2a.norm(4 * (i + 1)))
                         for i in range(8)]
                for dc in range(DC - 3):
                    jobs.append((dc + 3, lambda dc=dc: ln2b.stats(
                        dc + 1, chunk_fence=("ve", epi[dc]))))
                jobs.append((DC, lambda: (
                    ln2b.stats(DC - 2, chunk_fence=("ve", epi[DC - 3])),
                    ln2b.stats(DC - 1, chunk_fence=("ve", epi[DC - 2])),
                    ln2b.stats(DC, chunk_fence=("ve", epi[DC - 1])),
                    ln2b.rows_bcast(), ln2b.norm(DC))))
                return jobs

            ffn_pass(0, xready, None, tail_jobs=ln2_jobs)
            af, wa, hl3 = attention(ln2a.fence + ln2b.fence)
            if stop_after == "attn":
                dump_X(af)
            else:
                ffn_pass(1, af, wa, g0_hosted=hl3)
        ctx.wait("sync", "outd", ctx.cnt["outd"])

        @block.sync
        def _(e):
            for fn in ctx.prog["sync"]:
                fn(e)

        @block.scalar
        def _(e):
            for fn in ctx.prog["scalar"]:
                fn(e)

        @block.vector
        def _(e):
            for fn in ctx.prog["vector"]:
                fn(e)

        @block.tensor
        def _(e):
            for fn in ctx.prog["tensor"]:
                fn(e)

        @block.gpsimd
        def _(e):
            for fn in ctx.prog["gpsimd"]:
                fn(e)

    return nc



def _prep(L0_attn_x, L0_attn_Wqk_w, L0_attn_Wqk_b, L0_attn_Wv_w, L0_attn_Wv_b,
          L0_attn_Wo_w, L0_attn_Wo_b, L0_ffn_gate_w, L0_ffn_gate_s, L0_ffn_up_w,
          L0_ffn_up_s, L0_ffn_down_w, L0_ffn_down_s):
    f32 = np.float32
    x = np.asarray(L0_attn_x, dtype=f32).reshape(SEQ, D)
    gate_w = np.asarray(L0_ffn_gate_w, dtype=np.int32)
    up_w = np.asarray(L0_ffn_up_w, dtype=np.int32)
    down_w = np.asarray(L0_ffn_down_w, dtype=np.int32)
    Wqk = np.asarray(L0_attn_Wqk_w, dtype=f32)
    Wv = np.asarray(L0_attn_Wv_w, dtype=f32)
    Wo = np.asarray(L0_attn_Wo_w, dtype=f32)

    def blk(Wm, nt):
        nch = Wm.shape[1] // 128
        return np.ascontiguousarray(
            Wm.reshape(nt, 128, nch, 128).transpose(0, 3, 2, 1)
            .reshape(nt, 128, Wm.shape[1]))

    g_b = blk(gate_w, FT)
    u_b = blk(up_w, FT)
    bgu = np.ascontiguousarray(np.concatenate(
        [g_b.reshape(FT, 128, 1, D), u_b.reshape(FT, 128, 1, D)], axis=2)
        .reshape(FT, 128, 2 * D))
    bd = blk(down_w, DC)
    P0 = []
    for hp in range(NHP):
        qrows = np.concatenate([(2 * hp) * 128 + 2 * np.arange(E),
                                (2 * hp + 1) * 128 + 2 * np.arange(E)])
        P0.append(qrows)
        P0.append(qrows + 1)
    P0 = np.concatenate(P0)
    bqk = blk(Wqk[P0], 8)
    bvw = blk(Wv, 4)
    bow = blk(Wo, DC)
    gs2 = np.ascontiguousarray(np.asarray(L0_ffn_gate_s, f32).reshape(FT, 128).T)
    us2 = np.ascontiguousarray(np.asarray(L0_ffn_up_s, f32).reshape(FT, 128).T)
    ds2 = np.ascontiguousarray(np.asarray(L0_ffn_down_s, f32).reshape(DC, 128).T)
    qkb2 = np.ascontiguousarray(np.asarray(L0_attn_Wqk_b, f32)[P0].reshape(8, 128).T)
    vb2 = np.ascontiguousarray(np.asarray(L0_attn_Wv_b, f32).reshape(4, 128).T)
    ob2 = np.ascontiguousarray(np.asarray(L0_attn_Wo_b, f32).reshape(DC, 128).T)
    eyeD = np.eye(128, dtype=BF)
    onesbD = np.ones((128, 1), dtype=BF)
    onesfD = np.ones((1, 128), dtype=np.float32)

    shared = dict(bgu=bgu, bd=bd, bqk=bqk, bvw=bvw, bow=bow, gs2=gs2, us2=us2,
                  ds2=ds2, qkb2=qkb2, vb2=vb2, ob2=ob2, eyeD=eyeD, onesbD=onesbD,
                  onesfD=onesfD)
    in_maps = []
    for c in range(NCORES):
        m = dict(shared)
        m["xT"] = np.ascontiguousarray(x[c * TOK:(c + 1) * TOK, :].T)
        in_maps.append(m)
    return in_maps


def kernel(**inputs):
    stop_after = inputs.pop("_stop_after", "full")
    in_maps = _prep(**inputs)
    nc = build(stop_after=stop_after)
    res = run_bass_kernel_spmd(nc, in_maps, core_ids=list(range(NCORES)))
    outs = [np.asarray(res.results[c]["out"]) for c in range(NCORES)]
    full = np.concatenate([o.T for o in outs], axis=0)
    return np.ascontiguousarray(full.reshape(1, SEQ, D), dtype=np.float32)


# version 14
# speedup vs baseline: 1.0074x; 1.0074x over previous
import numpy as np
import ml_dtypes
from contextlib import ExitStack

import concourse.bass as bass
import concourse.mybir as mybir
from concourse.bass_utils import run_bass_kernel_spmd

F32 = mybir.dt.float32
F16 = mybir.dt.float16
BF16 = mybir.dt.bfloat16
I32 = mybir.dt.int32
AF = mybir.ActivationFunctionType
ALU = mybir.AluOpType
AX = mybir.AxisListType
BF = ml_dtypes.bfloat16

NCORES = 8
D = 4096
DC = D // 128
FF = 11008
FT = FF // 128
TOK = 1024
GRP = 512
NG = 2
NH = 8
E = 64
NHP = 4
NQT = 8
NKB = 16
NKC = 64
SEQ = NCORES * TOK
RSQ = float(512 ** -0.5)
EPS = 1e-5

XFREE = DC * TOK
HFREE = FT * GRP + 1024
H_Q = 0
H_OT = 4096
H_S = 8192
H_P = 24576
H_PT = 32768
H_KSTG = 8192
H_VSTG = 10240
H_CSTG = 12288
WFREE = 16384
GUSLOT = (0, 4096, 8192)
DNSLOT = (0, 5504)




class Ctx:
    def __init__(self, nc, es):
        self.nc = nc
        self.prog = {k: [] for k in ("sync", "scalar", "vector", "tensor", "gpsimd")}
        self.cnt = {}
        self.sem = {}
        for n in ("wg", "ws", "x", "scr", "kv", "sp", "cl", "outd", "dt",
                  "pe", "ac", "ve", "gp"):
            self.sem[n] = es.enter_context(nc.semaphore(n))
            self.cnt[n] = 0

    def emit(self, q, fn):
        self.prog[q].append(fn)

    def wait(self, q, name, val):
        if val is None or val <= 0:
            return
        s = self.sem[name]
        self.emit(q, lambda e, s=s, v=val: e.wait_ge(s, v))

    def waitf(self, q, fence):
        if fence is None:
            return
        if isinstance(fence, list):
            for f in fence:
                self.waitf(q, f)
            return
        self.wait(q, fence[0], fence[1])

    def op(self, q, fn, inc_sem=None, inc_by=1):
        if inc_sem is None:
            self.emit(q, fn)
            return None
        s = self.sem[inc_sem]
        v = self.cnt[inc_sem] + inc_by
        self.cnt[inc_sem] = v
        self.emit(q, lambda e, s=s, fn=fn, k=inc_by: fn(e).then_inc(s, k))
        return v


def build(stop_after="full"):
    nc = bass.Bass("TRN2", target_bir_lowering=False)

    xT = nc.dram_tensor("xT", [D, TOK], F32, kind="ExternalInput").ap()
    bgu = nc.dram_tensor("bgu", [FT, 128, 8192], I32, kind="ExternalInput").ap()
    bd = nc.dram_tensor("bd", [DC, 128, FF], I32, kind="ExternalInput").ap()
    bqk = nc.dram_tensor("bqk", [8, 128, 4096], F32, kind="ExternalInput").ap()
    bvw = nc.dram_tensor("bvw", [4, 128, 4096], F32, kind="ExternalInput").ap()
    bow = nc.dram_tensor("bow", [DC, 128, 512], F32, kind="ExternalInput").ap()
    gs2 = nc.dram_tensor("gs2", [128, FT], F32, kind="ExternalInput").ap()
    us2 = nc.dram_tensor("us2", [128, FT], F32, kind="ExternalInput").ap()
    ds2 = nc.dram_tensor("ds2", [128, DC], F32, kind="ExternalInput").ap()
    qkb2 = nc.dram_tensor("qkb2", [128, 8], F32, kind="ExternalInput").ap()
    vb2 = nc.dram_tensor("vb2", [128, 4], F32, kind="ExternalInput").ap()
    ob2 = nc.dram_tensor("ob2", [128, DC], F32, kind="ExternalInput").ap()
    eyeD = nc.dram_tensor("eyeD", [128, 128], BF16, kind="ExternalInput").ap()
    onesbD = nc.dram_tensor("onesbD", [128, 1], BF16, kind="ExternalInput").ap()
    onesfD = nc.dram_tensor("onesfD", [1, 128], F32, kind="ExternalInput").ap()
    out = nc.dram_tensor("out", [D, TOK], F32, kind="ExternalOutput").ap()

    wsc_gu = nc.dram_tensor("wsc_gu", [2 * FT, 128, 4096], BF16).ap()
    wsc_d = nc.dram_tensor("wsc_d", [2 * DC, 128, FF // 2], BF16).ap()
    xspill = nc.dram_tensor("xspill", [128, XFREE], BF16).ap()
    k_loc = nc.dram_tensor("k_loc", [512, TOK], BF16).ap()
    k_all = nc.dram_tensor("k_all", [NCORES * 512, TOK], BF16).ap()
    v_loc = nc.dram_tensor("v_loc", [128, 4096], BF16).ap()
    rows2 = nc.dram_tensor("rows2", [4, 512], F32).ap()
    v_all = nc.dram_tensor("v_all", [NCORES * 128, 4096], BF16).ap()

    es = ExitStack()
    with es:
        Xh = es.enter_context(nc.sbuf_tensor("X", [128, XFREE], BF16))
        Hh = es.enter_context(nc.sbuf_tensor("H", [128, HFREE], BF16))
        Wh = es.enter_context(nc.sbuf_tensor("W", [128, WFREE], BF16))
        X, H, W = Xh[:], Hh[:], Wh[:]
        Hf32 = Hh.bitcast(F32)[:]
        Hf16 = Hh.bitcast(F16)[:]
        Wf32 = Wh.bitcast(F32)[:]
        gs = es.enter_context(nc.sbuf_tensor("gs", [128, FT], F32))[:]
        us = es.enter_context(nc.sbuf_tensor("us", [128, FT], F32))[:]
        ds = es.enter_context(nc.sbuf_tensor("ds", [128, DC], F32))[:]
        dsh = es.enter_context(nc.sbuf_tensor("dsh", [128, DC], F32))[:]
        qkb = es.enter_context(nc.sbuf_tensor("qkb", [128, 8], F32))[:]
        vbb = es.enter_context(nc.sbuf_tensor("vbb", [128, 4], F32))[:]
        obb = es.enter_context(nc.sbuf_tensor("obb", [128, DC], F32))[:]
        eye = es.enter_context(nc.sbuf_tensor("eye", [128, 128], BF16))[:]
        onesb = es.enter_context(nc.sbuf_tensor("onesb", [128, 1], BF16))[:]
        onesf = es.enter_context(nc.sbuf_tensor("onesf", [1, 128], F32))[:]
        bc0 = es.enter_context(nc.sbuf_tensor("bc0", [128, GRP], F32))[:]
        bc1 = es.enter_context(nc.sbuf_tensor("bc1", [128, GRP], F32))[:]
        sqt = es.enter_context(nc.sbuf_tensor("sqt", [128, 2 * GRP], BF16))[:]
        sgt = es.enter_context(nc.sbuf_tensor("sgt", [128, 2 * GRP], BF16))[:]
        sut = es.enter_context(nc.sbuf_tensor("sut", [128, GRP], BF16))[:]
        ett = es.enter_context(nc.sbuf_tensor("ett", [128, GRP], F32))[:]
        ott = es.enter_context(nc.sbuf_tensor("ott", [128, 2 * GRP], F32))[:]
        rows = es.enter_context(nc.sbuf_tensor("rows", [1, 1536], F32))[:]
        negm = es.enter_context(nc.sbuf_tensor("negm", [128, 1], F32))[:]
        lcol = es.enter_context(nc.sbuf_tensor("lcol", [128, 1], F32))[:]
        invl = es.enter_context(nc.sbuf_tensor("invl", [128, 2], F32))[:]

        B = [es.enter_context(nc.psum_tensor(f"B{i}", [128, 512], F32))[:] for i in range(6)]
        PBF = es.enter_context(nc.psum_tensor("PBF", [128, 1024], BF16))[:]

        ctx = Ctx(nc, es)
        block = es.enter_context(nc.Block())

        r_m = rows[0:1, 0:512]
        r_q = rows[0:1, 512:1024]
        r_t = rows[0:1, 1024:1536]

        def Xg(dc, g):
            return X[:, dc * TOK + g * GRP:(dc * TOK + (g + 1) * GRP)]

        bankstate = [None] * 6
        pbf_state = None
        bc_users = None

        for (dst, src) in ((gs, gs2), (us, us2), (ds, ds2), (qkb, qkb2), (vbb, vb2),
                           (obb, ob2), (eye, eyeD), (onesb, onesbD), (onesf, onesfD)):
            ctx.op("sync", lambda e, d=dst, s=src: e.dma_start(out=d, in_=s), "ws", 16)
        consts_ws = ctx.cnt["ws"]
        for q4 in range(4):
            ctx.op("gpsimd", lambda e, q4=q4: e.dma_start(
                out=X.rearrange("p (dc t) -> p dc t", dc=DC)[:, q4 * 8:(q4 + 1) * 8],
                in_=xT.rearrange("(dc p) t -> p dc t", p=128)[:, q4 * 8:(q4 + 1) * 8]),
                "x", 16)
        xready = ("x", ctx.cnt["x"])
        ctx.wait("tensor", "ws", consts_ws)
        ctx.wait("scalar", "ws", consts_ws)
        ctx.wait("gpsimd", "ws", consts_ws)
        ctx.wait("vector", "ws", consts_ws)
        ctx.op("vector", lambda e: e.tensor_scalar_mul(dsh, ds, 0.5), "ve")

        def ln_group(g, xfence, keep_rows=None, gp_ok=True, xchunked=False):
            nonlocal bankstate, bc_users
            if not xchunked:
                ctx.waitf("tensor", xfence)
                ctx.waitf("vector", xfence)
            for b in (4, 5):
                ctx.waitf("tensor", bankstate[b])
            sq_fence = [None, None]
            for dc in range(DC):
                buf = dc % 2
                if xchunked:
                    ctx.wait("tensor", "x", 16 * (dc // 8 + 1))
                    ctx.wait("vector", "x", 16 * (dc // 8 + 1))
                if sq_fence[buf] is not None:
                    ctx.wait("vector", "pe", sq_fence[buf])
                ctx.op("vector", lambda e, dc=dc, buf=buf: e.tensor_mul(
                    sqt[:, buf * GRP:(buf + 1) * GRP], Xg(dc, g), Xg(dc, g)), "ve")
                vsq = ctx.cnt["ve"]
                ctx.op("tensor", lambda e, dc=dc: e.matmul(
                    B[4][0:1, :], onesb, Xg(dc, g),
                    start=(dc == 0), stop=(dc == DC - 1)), "pe")
                ctx.wait("tensor", "ve", vsq)
                ctx.op("tensor", lambda e, dc=dc, buf=buf: e.matmul(
                    B[5][0:1, :], onesb, sqt[:, buf * GRP:(buf + 1) * GRP],
                    start=(dc == 0), stop=(dc == DC - 1)), "pe")
                sq_fence[buf] = ctx.cnt["pe"]
            pe_stats = ctx.cnt["pe"]
            ctx.wait("scalar", "pe", pe_stats)
            ctx.op("scalar", lambda e: e.mul(r_m, B[4][0:1, :], 1.0 / D), "ac")
            ctx.op("scalar", lambda e: e.mul(r_q, B[5][0:1, :], 1.0 / D), "ac")
            ac_rows = ctx.cnt["ac"]
            ctx.wait("vector", "ac", ac_rows)
            ctx.op("vector", lambda e: e.tensor_mul(r_t, r_m, r_m), "ve")
            ctx.op("vector", lambda e: e.tensor_sub(r_q, r_q, r_t), "ve")
            ctx.op("vector", lambda e: e.tensor_scalar_add(r_q, r_q, EPS), "ve")
            ve_var = ctx.cnt["ve"]
            ctx.wait("scalar", "ve", ve_var)
            ctx.op("scalar", lambda e: e.activation(r_q, r_q, AF.Ln), "ac")
            ctx.op("scalar", lambda e: e.activation(r_t, r_q, AF.Exp, scale=-0.5), "ac")
            ctx.op("scalar", lambda e: e.activation(r_q, r_q, AF.Exp, scale=0.5), "ac")
            if keep_rows is not None:
                g2 = keep_rows
                ctx.wait("scalar", "ac", ctx.cnt["ac"])
                ctx.op("scalar", lambda e, g2=g2: e.dma_start(
                    out=rows2[g2:g2 + 1, :], in_=r_m), "sp", 16)
                ctx.op("scalar", lambda e, g2=g2: e.dma_start(
                    out=rows2[2 + g2:3 + g2, :], in_=r_q), "sp", 16)
            ac_rsd = ctx.cnt["ac"]
            ctx.wait("tensor", "ac", ac_rsd)
            ctx.op("tensor", lambda e: e.matmul(B[4][:, :], onesf, r_t,
                                                start=True, stop=True), "pe")
            ctx.op("tensor", lambda e: e.matmul(B[5][:, :], onesf, r_m,
                                                start=True, stop=True), "pe")
            pe_bc = ctx.cnt["pe"]
            ctx.wait("scalar", "pe", pe_bc)
            ctx.waitf("scalar", bc_users)
            ctx.op("scalar", lambda e: e.copy(bc0, B[4][:, :]), "ac")
            ctx.op("scalar", lambda e: e.copy(bc1, B[5][:, :]), "ac")
            ac_bc = ctx.cnt["ac"]
            bankstate[4] = ("ac", ac_bc)
            bankstate[5] = ("ac", ac_bc)
            ctx.wait("vector", "ac", ac_bc)
            if gp_ok:
                ctx.wait("gpsimd", "ac", ac_bc)
            for dc in range(DC):
                buf = dc % 2
                q = "vector" if (dc % 2 == 0 or not gp_ok) else "gpsimd"
                sm = "ve" if q == "vector" else "gp"
                ctx.op(q, lambda e, dc=dc, buf=buf: e.tensor_sub(
                    sqt[:, buf * GRP:(buf + 1) * GRP], Xg(dc, g), bc1), sm)
                ctx.op(q, lambda e, dc=dc, buf=buf: e.tensor_mul(
                    Xg(dc, g), sqt[:, buf * GRP:(buf + 1) * GRP], bc0), sm)
            out = [("ve", ctx.cnt["ve"])]
            if gp_ok:
                out.append(("gp", ctx.cnt["gp"]))
            return out

        class HLN:

            def __init__(self, g, bs, bq, keep_slot=None, scratch=None):
                self.g, self.bs, self.bq = g, bs, bq
                self.keep_slot = keep_slot
                self.scratch = scratch
                self.started = False
                self.nstat = 0
                self.sq_fence = [None, None]
                self.fence = None

            def stats(self, upto, chunk_fence=None):
                g, bs, bq = self.g, self.bs, self.bq
                if not self.started:
                    ctx.waitf("tensor", bankstate[bs])
                    ctx.waitf("tensor", bankstate[bq])
                    self.started = True
                for dc in range(self.nstat, upto):
                    buf = dc % 2 if self.scratch is None else 0
                    sqv = sqt[:, buf * GRP:(buf + 1) * GRP] \
                        if self.scratch is None else self.scratch
                    if chunk_fence is not None:
                        ctx.wait("vector", chunk_fence[0], chunk_fence[1])
                        ctx.wait("tensor", chunk_fence[0], chunk_fence[1])
                    if self.sq_fence[buf] is not None:
                        ctx.wait("vector", "pe", self.sq_fence[buf])
                    ctx.op("vector", lambda e, dc=dc, g=g, sqv=sqv: e.tensor_mul(
                        sqv, Xg(dc, g), Xg(dc, g)), "ve")
                    vsq = ctx.cnt["ve"]
                    ctx.op("tensor", lambda e, dc=dc, g=g, bs=bs: e.matmul(
                        B[bs][0:1, :], onesb, Xg(dc, g),
                        start=(dc == 0), stop=(dc == DC - 1)), "pe")
                    ctx.wait("tensor", "ve", vsq)
                    ctx.op("tensor", lambda e, dc=dc, bq=bq, sqv=sqv: e.matmul(
                        B[bq][0:1, :], onesb, sqv,
                        start=(dc == 0), stop=(dc == DC - 1)), "pe")
                    self.sq_fence[buf] = ctx.cnt["pe"]
                self.nstat = upto

            def rows_bcast(self):
                bs, bq = self.bs, self.bq
                pe_stats = ctx.cnt["pe"]
                ctx.wait("scalar", "pe", pe_stats)
                ctx.op("scalar", lambda e, bs=bs: e.mul(r_m, B[bs][0:1, :], 1.0 / D), "ac")
                ctx.op("scalar", lambda e, bq=bq: e.mul(r_q, B[bq][0:1, :], 1.0 / D), "ac")
                ac_rows = ctx.cnt["ac"]
                ctx.wait("vector", "ac", ac_rows)
                ctx.op("vector", lambda e: e.tensor_mul(r_t, r_m, r_m), "ve")
                ctx.op("vector", lambda e: e.tensor_sub(r_q, r_q, r_t), "ve")
                ctx.op("vector", lambda e: e.tensor_scalar_add(r_q, r_q, EPS), "ve")
                ve_var = ctx.cnt["ve"]
                ctx.wait("scalar", "ve", ve_var)
                ctx.op("scalar", lambda e: e.activation(r_q, r_q, AF.Ln), "ac")
                ctx.op("scalar", lambda e: e.activation(r_t, r_q, AF.Exp, scale=-0.5), "ac")
                ctx.op("scalar", lambda e: e.activation(r_q, r_q, AF.Exp, scale=0.5), "ac")
                if self.keep_slot is not None:
                    g2 = self.keep_slot
                    ctx.wait("scalar", "ac", ctx.cnt["ac"])
                    ctx.op("scalar", lambda e, g2=g2: e.dma_start(
                        out=rows2[g2:g2 + 1, :], in_=r_m), "sp", 16)
                    ctx.op("scalar", lambda e, g2=g2: e.dma_start(
                        out=rows2[2 + g2:3 + g2, :], in_=r_q), "sp", 16)
                ac_rsd = ctx.cnt["ac"]
                ctx.wait("tensor", "ac", ac_rsd)
                ctx.op("tensor", lambda e, bs=bs: e.matmul(
                    B[bs][:, :], onesf, r_t, start=True, stop=True), "pe")
                ctx.op("tensor", lambda e, bq=bq: e.matmul(
                    B[bq][:, :], onesf, r_m, start=True, stop=True), "pe")
                self.pe_bc = ctx.cnt["pe"]
                self.nnorm = 0

            def norm(self, upto):
                g, bs, bq = self.g, self.bs, self.bq
                if self.nnorm == 0:
                    ctx.wait("vector", "pe", self.pe_bc)
                for dc in range(self.nnorm, upto):
                    buf = dc % 2 if self.scratch is None else 0
                    sqv = sqt[:, buf * GRP:(buf + 1) * GRP] \
                        if self.scratch is None else self.scratch
                    ctx.op("vector", lambda e, dc=dc, g=g, bq=bq, sqv=sqv: e.tensor_sub(
                        sqv, Xg(dc, g), B[bq][:, :]), "ve")
                    ctx.op("vector", lambda e, dc=dc, g=g, bs=bs, sqv=sqv: e.tensor_mul(
                        Xg(dc, g), sqv, B[bs][:, :]), "ve")
                self.nnorm = upto
                if upto == DC:
                    v = ctx.cnt["ve"]
                    bankstate[self.bs] = ("ve", v)
                    bankstate[self.bq] = ("ve", v)
                    self.fence = [("ve", v)]

        def ffn_pass(pass_idx, xfence, warena_fence, tail_jobs=None, g0_hosted=None):
            nonlocal bankstate, bc_users
            guslot_free = [warena_fence] * 3
            dnslot_free = [warena_fence] * 2
            ott_free = [0, 0]
            epi_ve = [[0] * DC, [0] * DC]
            hosted_next = None
            for g in range(NG):
                first = (pass_idx == 0 and g == 0)
                scratch_ready = ctx.cnt["scr"]
                if g == 0 and g0_hosted is not None:
                    xln_f = g0_hosted.fence
                elif hosted_next is not None:
                    xln_f = hosted_next.fence
                    hosted_next = None
                else:
                    xln_f = ln_group(g, xfence if g == 0 else None,
                                     gp_ok=not (pass_idx == 0 and g == 0),
                                     xchunked=(pass_idx == 0 and g == 0))
                h_fence = 0
                scr_waited = False
                for ft in range(FT):
                    wv = [None, None]
                    for mat in range(2):
                        slot = (2 * ft + mat) % 3
                        int32_src = pass_idx == 0 and (g == 0 or mat == 1)
                        write_scr = pass_idx == 0 and ((g == 0 and mat == 0) or
                                                       (g == 1 and mat == 1))
                        if int32_src:
                            ctx.waitf("gpsimd", guslot_free[slot])
                            v = ctx.op("gpsimd", lambda e, ft=ft, mat=mat, slot=slot:
                                       e.dma_start(
                                           out=W[:, GUSLOT[slot]:GUSLOT[slot] + 4096],
                                           in_=bgu[ft, :, mat * 4096:(mat + 1) * 4096]),
                                       "wg", 16)
                            wv[mat] = ("wg", v)
                            guslot_free[slot] = []
                            if write_scr:
                                ctx.wait("sync", "wg", v)
                                sv = ctx.op("sync", lambda e, ft=ft, mat=mat, slot=slot:
                                            e.dma_start(
                                                out=wsc_gu[2 * ft + mat],
                                                in_=W[:, GUSLOT[slot]:GUSLOT[slot] + 4096]),
                                            "scr", 16)
                                guslot_free[slot] = [("scr", sv)]
                        else:
                            ctx.waitf("sync", guslot_free[slot])
                            if not scr_waited and pass_idx == 0 and g == 1:
                                ctx.wait("sync", "scr", scratch_ready)
                                scr_waited = True
                            v = ctx.op("sync", lambda e, ft=ft, mat=mat, slot=slot:
                                       e.dma_start(
                                           out=W[:, GUSLOT[slot]:GUSLOT[slot] + 4096],
                                           in_=wsc_gu[2 * ft + mat]), "ws", 16)
                            wv[mat] = ("ws", v)
                            guslot_free[slot] = []
                    gb, ub = (0, 1) if ft % 2 == 0 else (2, 3)
                    sgi, sui = (2 * ft) % 3, (2 * ft + 1) % 3
                    ctx.waitf("tensor", bankstate[gb])
                    ctx.waitf("tensor", wv[0])
                    if ft == 0:
                        ctx.waitf("tensor", xln_f)
                    for dc in range(DC):
                        ctx.op("tensor", lambda e, dc=dc, slot=sgi, gb=gb, g=g: e.matmul(
                            B[gb][:, :],
                            W[:, GUSLOT[slot] + dc * 128:GUSLOT[slot] + (dc + 1) * 128],
                            Xg(dc, g), start=(dc == 0), stop=(dc == DC - 1)), "pe")
                    pe_g = ctx.cnt["pe"]
                    guslot_free[sgi] = (guslot_free[sgi] if isinstance(guslot_free[sgi], list)
                                        else []) + [("pe", pe_g)]
                    ctx.waitf("tensor", bankstate[ub])
                    ctx.waitf("tensor", wv[1])
                    for dc in range(DC):
                        ctx.op("tensor", lambda e, dc=dc, slot=sui, ub=ub, g=g: e.matmul(
                            B[ub][:, :],
                            W[:, GUSLOT[slot] + dc * 128:GUSLOT[slot] + (dc + 1) * 128],
                            Xg(dc, g), start=(dc == 0), stop=(dc == DC - 1)), "pe")
                    pe_u = ctx.cnt["pe"]
                    guslot_free[sui] = (guslot_free[sui] if isinstance(guslot_free[sui], list)
                                        else []) + [("pe", pe_u)]
                    eb = ft % 2
                    ctx.wait("scalar", "pe", pe_g)
                    ac_g = ctx.op("scalar", lambda e, ft=ft, eb=eb, gb=gb: e.activation(
                        sgt[:, eb * GRP:(eb + 1) * GRP], B[gb][:, :], AF.Silu,
                        scale=gs[:, ft:ft + 1]), "ac")
                    bankstate[gb] = ("ac", ac_g)
                    ctx.wait("vector", "pe", pe_u)
                    ctx.op("vector", lambda e, ft=ft, eb=eb, ub=ub: e.tensor_scalar(
                        sut, B[ub][:, :], us[:, ft:ft + 1],
                        None, ALU.mult), "ve")
                    bankstate[ub] = ("ve", ctx.cnt["ve"])
                    ctx.wait("vector", "ac", ac_g)
                    ctx.op("vector", lambda e, ft=ft, eb=eb: e.tensor_mul(
                        H[:, ft * GRP:(ft + 1) * GRP], sgt[:, eb * GRP:(eb + 1) * GRP],
                        sut), "ve")
                    h_fence = ctx.cnt["ve"]
                gu_end = ("pe", ctx.cnt["pe"])
                dn_extra = [f for s in guslot_free if isinstance(s, list) for f in s]
                dnslot_free = [dnslot_free[0] if g == 0 and pass_idx == 0 else None,
                               None]
                dnslot_free = [dn_extra + [gu_end], dn_extra + [gu_end]]
                ctx.waitf("tensor", bankstate[4])
                ctx.waitf("tensor", bankstate[5])
                ctx.op("tensor", lambda e: e.matmul(B[4][:, :], onesf, r_q,
                                                    start=True, stop=True), "pe")
                ctx.op("tensor", lambda e: e.matmul(B[5][:, :], onesf, r_m,
                                                    start=True, stop=True), "pe")
                pe_bc = ctx.cnt["pe"]
                ctx.wait("scalar", "pe", pe_bc)
                ctx.waitf("scalar", bc_users)
                ctx.op("scalar", lambda e: e.copy(bc0, B[4][:, :]), "ac")
                ac_sdb = ctx.op("scalar", lambda e: e.copy(bc1, B[5][:, :]), "ac")
                bankstate[4] = ("ac", ac_sdb)
                bankstate[5] = ("ac", ac_sdb)
                ctx.wait("vector", "ac", ac_sdb)
                ctx.waitf("vector", xln_f)
                jobs = []
                if g == 0:
                    hosted_next = HLN(1, 4, 5)
                    hl = hosted_next
                    jobs += [(14, lambda hl=hl: hl.stats(16)),
                             (16, lambda hl=hl: (hl.stats(DC), hl.rows_bcast()))]
                    jobs += [(18 + i, lambda hl=hl, i=i: hl.norm(4 * (i + 1)))
                             for i in range(8)]
                if g == NG - 1 and tail_jobs is not None:
                    jobs += tail_jobs(epi_ve[g])
                jobs.sort(key=lambda t: t[0])
                for j in range(DC):
                    for (jj, fn) in jobs:
                        if jj == j:
                            fn()
                    wvh = [None, None]
                    for hh in range(2):
                        slot = hh
                        if pass_idx == 0:
                            ctx.waitf("gpsimd", dnslot_free[slot])
                            v = ctx.op("gpsimd", lambda e, j=j, hh=hh, slot=slot:
                                       e.dma_start(
                                           out=W[:, DNSLOT[slot]:DNSLOT[slot] + 5504],
                                           in_=bd[j, :, hh * 5504:(hh + 1) * 5504]),
                                       "wg", 16)
                            wvh[hh] = ("wg", v)
                            dnslot_free[slot] = []
                            if g == 1:
                                ctx.wait("sync", "wg", v)
                                sv = ctx.op("sync", lambda e, j=j, hh=hh, slot=slot:
                                            e.dma_start(
                                                out=wsc_d[2 * j + hh],
                                                in_=W[:, DNSLOT[slot]:DNSLOT[slot] + 5504]),
                                            "scr", 16)
                                dnslot_free[slot] = [("scr", sv)]
                        else:
                            ctx.waitf("gpsimd", dnslot_free[slot])
                            if j == 0 and hh == 0:
                                ctx.wait("gpsimd", "scr", scratch_ready)
                            v = ctx.op("gpsimd", lambda e, j=j, hh=hh, slot=slot:
                                       e.dma_start(
                                           out=W[:, DNSLOT[slot]:DNSLOT[slot] + 5504],
                                           in_=wsc_d[2 * j + hh]), "wg", 16)
                            wvh[hh] = ("wg", v)
                            dnslot_free[slot] = []
                    db = j % 2
                    ctx.waitf("tensor", bankstate[db])
                    if j == 0:
                        ctx.wait("tensor", "ve", h_fence)
                    for fc in range(FT):
                        hh = fc // 43
                        if fc == 0:
                            ctx.waitf("tensor", wvh[0])
                        elif fc == 43:
                            ctx.waitf("tensor", wvh[1])
                        ctx.op("tensor", lambda e, fc=fc, hh=hh, db=db: e.matmul(
                            B[db][:, :],
                            W[:, DNSLOT[hh] + (fc - hh * 43) * 128:
                                 DNSLOT[hh] + (fc - hh * 43 + 1) * 128],
                            H[:, fc * GRP:(fc + 1) * GRP],
                            start=(fc == 0), stop=(fc == FT - 1)), "pe")
                        if fc == 42:
                            dnslot_free[0] = (dnslot_free[0] if isinstance(dnslot_free[0], list)
                                              else []) + [("pe", ctx.cnt["pe"])]
                    pe_d = ctx.cnt["pe"]
                    dnslot_free[1] = (dnslot_free[1] if isinstance(dnslot_free[1], list)
                                      else []) + [("pe", pe_d)]
                    eb = j % 2
                    ctx.wait("vector", "pe", pe_d)
                    ctx.op("vector", lambda e, j=j, eb=eb, db=db: e.scalar_tensor_tensor(
                        ett, B[db][:, :], dsh[:, j:j + 1],
                        bc1, ALU.mult, ALU.add), "ve")
                    bankstate[db] = ("ve", ctx.cnt["ve"])
                    ctx.op("vector", lambda e, j=j, eb=eb, g=g: e.tensor_mul(
                        sgt[:, eb * GRP:(eb + 1) * GRP], Xg(j, g), bc0), "ve")
                    if pass_idx == 0:
                        ctx.op("vector", lambda e, j=j, eb=eb, g=g: e.tensor_add(
                            Xg(j, g), ett,
                            sgt[:, eb * GRP:(eb + 1) * GRP]), "ve")
                        epi_ve[g][j] = ctx.cnt["ve"]
                    else:
                        ctx.wait("vector", "outd", ott_free[eb])
                        vout = ctx.op("vector", lambda e, j=j, eb=eb: e.tensor_add(
                            ott[:, eb * GRP:(eb + 1) * GRP],
                            ett,
                            sgt[:, eb * GRP:(eb + 1) * GRP]), "ve")
                        ctx.wait("scalar", "ve", vout)
                        ctx.op("scalar", lambda e, j=j, eb=eb, g=g: e.dma_start(
                            out=out.rearrange("(j p) t -> j p t", p=128)[j]
                                [:, g * GRP:(g + 1) * GRP],
                            in_=ott[:, eb * GRP:(eb + 1) * GRP]), "outd", 16)
                        ott_free[eb] = ctx.cnt["outd"]
                for (jj, fn) in jobs:
                    if jj >= DC:
                        fn()
                dn_end = [f for s in dnslot_free if isinstance(s, list) for f in s]
                guslot_free = [dn_end, dn_end, dn_end]
                bc_users = ("ve", ctx.cnt["ve"])
            return ("ve", ctx.cnt["ve"])

        def dump_X(fence):
            ctx.waitf("scalar", fence)
            slot_out = [0, 0]
            for dc in range(DC):
                eb = dc % 2
                ctx.wait("scalar", "outd", ctx.cnt["outd"])
                v1 = ctx.op("scalar", lambda e, dc=dc, eb=eb: e.copy(
                    ott[:, eb * GRP:(eb + 1) * GRP], X[:, dc * TOK:dc * TOK + GRP]), "ac")
                v2 = ctx.op("scalar", lambda e, dc=dc, eb=eb: e.copy(
                    ett,
                    X[:, dc * TOK + GRP:dc * TOK + TOK]), "ac")
                ctx.wait("sync", "ac", v2)
                ctx.op("sync", lambda e, dc=dc, eb=eb: e.dma_start(
                    out=out.rearrange("(j p) t -> j p t", p=128)[dc][:, 0:GRP],
                    in_=ott[:, eb * GRP:(eb + 1) * GRP]), "outd", 16)
                ctx.op("sync", lambda e, dc=dc, eb=eb: e.dma_start(
                    out=out.rearrange("(j p) t -> j p t", p=128)[dc][:, GRP:TOK],
                    in_=ett), "outd", 16)
                slot_out[eb] = ctx.cnt["outd"]

        def attention(xln2_f):
            nonlocal bankstate, pbf_state, bc_users
            ffn1_pe = ctx.cnt["pe"]
            ctx.waitf("scalar", xln2_f)
            sp_v = ctx.op("scalar", lambda e: e.dma_start(out=xspill, in_=X), "sp", 16)

            kstg = [H[:, H_KSTG + i * TOK:H_KSTG + (i + 1) * TOK] for i in range(2)]
            vstg = [H[:, H_VSTG + i * TOK:H_VSTG + (i + 1) * TOK] for i in range(2)]
            cstg = H[:, H_CSTG:H_CSTG + 4096]
            Q = H[:, H_Q:H_Q + 4096]
            OT = H[:, H_OT:H_OT + 4096]
            Sb = [Hf16[:, H_S:H_S + 8192], Hf16[:, H_S + 8192:H_S + 16384]]
            Pp = [H[:, H_P:H_P + 8192], H[:, H_P + 8192:H_P + 16384]]
            PTb = [W[:, 0:8192], W[:, 8192:16384]]

            wslab_b = [W[:, 8192:12288], W[:, 12288:16384]]
            ffn_done = ("pe", ffn1_pe)
            slab_prev = ffn_done
            kstg_free = [ffn_done, ffn_done]
            vstg_free = [ffn_done, ffn_done]
            qkv_pe = 0
            cstg_ve = 0
            kv_last = 0
            first_cstg = True
            qkv_pe_hist = [0, 0]
            slab_v = [0] * 12
            for ht in range(12):
                sl2 = (ht + 1) % 2
                if qkv_pe_hist[sl2]:
                    ctx.wait("gpsimd", "pe", qkv_pe_hist[sl2])
                elif sl2 == 0:
                    ctx.waitf("gpsimd", ffn_done)
                srcw = bqk[ht] if ht < 8 else bvw[ht - 8]
                slab_v[ht] = ctx.op("gpsimd", lambda e, srcw=srcw, sl2=sl2: e.dma_start(
                    out=wslab_b[sl2], in_=srcw), "wg", 16)
                for th in range(2):
                    qb = 2 * (ht % 2) + th
                    ctx.waitf("tensor", bankstate[qb])
                    ctx.wait("tensor", "wg", slab_v[ht])
                    if ht == 0 and th == 0:
                        ctx.waitf("tensor", xln2_f)
                    for dc in range(DC):
                        ctx.op("tensor", lambda e, dc=dc, th=th, qb=qb, sl2=sl2: e.matmul(
                            B[qb][:, :], wslab_b[sl2][:, dc * 128:(dc + 1) * 128],
                            Xg(dc, th), start=(dc == 0), stop=(dc == DC - 1)), "pe")
                    pe_v = ctx.cnt["pe"]
                    qkv_pe = pe_v
                    qkv_pe_hist[sl2] = pe_v
                    ctx.wait("scalar", "pe", pe_v)
                    if ht < 8:
                        hp, qk = ht // 2, ht % 2
                        if qk == 0:
                            av = ctx.op("scalar", lambda e, hp=hp, th=th, qb=qb:
                                        e.activation(
                                            Q[:, hp * TOK + th * GRP:
                                              hp * TOK + (th + 1) * GRP],
                                            B[qb][:, :], AF.Identity,
                                            bias=qkb[:, 2 * hp:2 * hp + 1]), "ac")
                        else:
                            kb_i = hp % 2
                            ctx.waitf("scalar", kstg_free[kb_i])
                            av = ctx.op("scalar", lambda e, th=th, qb=qb, kb_i=kb_i, hp=hp:
                                        e.activation(
                                            kstg[kb_i][:, th * GRP:(th + 1) * GRP],
                                            B[qb][:, :], AF.Identity,
                                            bias=qkb[:, 2 * hp + 1:2 * hp + 2]), "ac")
                    else:
                        vt = ht - 8
                        vb_i = vt % 2
                        ctx.waitf("scalar", vstg_free[vb_i])
                        av = ctx.op("scalar", lambda e, th=th, qb=qb, vt=vt, vb_i=vb_i:
                                    e.activation(
                                        vstg[vb_i][:, th * GRP:(th + 1) * GRP],
                                        B[qb][:, :], AF.Identity,
                                        bias=vbb[:, vt:vt + 1]), "ac")
                    bankstate[qb] = ("ac", av)
                if ht < 8 and ht % 2 == 1:
                    hp = ht // 2
                    kb_i = hp % 2
                    acv = ctx.cnt["ac"]
                    ctx.wait("sync", "ac", acv)
                    wv = ctx.op("sync", lambda e, hp=hp, kb_i=kb_i: e.dma_start(
                        out=k_loc.rearrange("(hp p) t -> hp p t", p=128)[hp],
                        in_=kstg[kb_i]), "kv", 16)
                    kstg_free[kb_i] = ("kv", wv)
                    kv_last = wv
                    if ht == 7:
                        ctx.wait("gpsimd", "kv", kv_last)
                        ctx.op("gpsimd", lambda e: e.collective_compute(
                            "AllGather", ALU.bypass,
                            replica_groups=[list(range(NCORES))],
                            ins=[k_loc], outs=[k_all]), "cl")
                if ht >= 8:
                    vt = ht - 8
                    vb_i = vt % 2
                    acv = ctx.cnt["ac"]
                    ctx.wait("tensor", "ac", acv)
                    if first_cstg:
                        ctx.waitf("vector", ffn_done)
                        first_cstg = False
                    for tc in range(8):
                        ctx.waitf("tensor", pbf_state)
                        ctx.op("tensor", lambda e, tc=tc, vb_i=vb_i: e.transpose(
                            PBF[:, tc * 128:(tc + 1) * 128],
                            vstg[vb_i][:, tc * 128:(tc + 1) * 128], eye), "pe")
                        tv = ctx.cnt["pe"]
                        ctx.wait("vector", "pe", tv)
                        ctx.op("vector", lambda e, vt=vt, tc=tc: e.tensor_copy(
                            cstg[:, ((2 * vt) * 8 + tc) * 64:
                                 ((2 * vt) * 8 + tc) * 64 + 64],
                            PBF[:, tc * 128:tc * 128 + 64]), "ve")
                        ctx.op("vector", lambda e, vt=vt, tc=tc: e.tensor_copy(
                            cstg[:, ((2 * vt + 1) * 8 + tc) * 64:
                                 ((2 * vt + 1) * 8 + tc) * 64 + 64],
                            PBF[:, tc * 128 + 64:(tc + 1) * 128]), "ve")
                        pbf_state = ("ve", ctx.cnt["ve"])
                    vstg_free[vb_i] = ("pe", ctx.cnt["pe"])
                    cstg_ve = ctx.cnt["ve"]
            ctx.wait("sync", "ve", cstg_ve)
            kv_last = ctx.op("sync", lambda e: e.dma_start(out=v_loc, in_=cstg), "kv", 16)
            ctx.wait("gpsimd", "kv", kv_last)
            ctx.op("gpsimd", lambda e: e.collective_compute(
                "AllGather", ALU.bypass, replica_groups=[list(range(NCORES))],
                ins=[v_loc], outs=[v_all]), "cl")
            cl_done = ctx.cnt["cl"]
            ctx.wait("gpsimd", "cl", cl_done - 1)
            ctx.wait("gpsimd", "sp", sp_v)
            ctx.wait("gpsimd", "pe", qkv_pe)
            kload = [None] * 4
            for hp in range(NHP):
                v = ctx.op("gpsimd", lambda e, hp=hp: e.dma_start(
                    out=X[:, hp * 8192:(hp + 1) * 8192].rearrange(
                        "p (r t) -> p r t", r=NCORES),
                    in_=k_all.rearrange("(r b p) t -> b p r t", r=NCORES, b=NHP)[hp]),
                    "kv", 16)
                kload[hp] = ("kv", v)
            vbuf = H[:, 40960:45056]
            vbuf_free = [("pe", qkv_pe)]
            vbuf_loaded = None

            def load_vbuf(h):
                nonlocal vbuf_loaded
                ctx.waitf("gpsimd", vbuf_free)
                ctx.wait("gpsimd", "cl", cl_done)
                v = ctx.op("gpsimd", lambda e, h=h: e.dma_start(
                    out=vbuf.rearrange("p (r f) -> p r f", r=NCORES),
                    in_=v_all.rearrange("(r p) f -> p r f", p=128)
                        [:, :, h * 512:(h + 1) * 512]), "kv", 16)
                vbuf_loaded = ("kv", v)

            load_vbuf(0)
            sc_banks = [None, None, None]
            pv_bank = None
            pt_free = [None, None]
            ot_slot = [None, None]
            ex = [0] * 65
            iv = [0] * 65
            dtv = [0] * 65
            q_ready = ("ac", ctx.cnt["ac"])

            def emit_scores(i):
                h, qt = divmod(i, NQT)
                hp, hh = h // 2, h % 2
                sb = i % 2
                if qt == 0:
                    ctx.waitf("tensor", kload[hp])
                    ctx.waitf("tensor", q_ready)
                for kb in range(NKB):
                    bi = kb % 3
                    ctx.waitf("tensor", sc_banks[bi])
                    ctx.op("tensor", lambda e, hp=hp, hh=hh, qt=qt, kb=kb, bi=bi:
                           e.matmul(
                               B[bi][:, :],
                               Q[hh * 64:(hh + 1) * 64,
                                 hp * TOK + qt * 128:hp * TOK + (qt + 1) * 128],
                               X[hh * 64:(hh + 1) * 64,
                                 hp * 8192 + kb * 512:hp * 8192 + (kb + 1) * 512],
                               start=True, stop=True), "pe")
                    pe_s = ctx.cnt["pe"]
                    if kb % 16 in (0, 3, 6, 9, 12, 15):
                        ctx.wait("scalar", "pe", pe_s)
                        if kb == 0 and i >= 2:
                            ctx.wait("scalar", "ac", ex[i - 2])
                        v = ctx.op("scalar", lambda e, kb=kb, bi=bi, sb=sb: e.copy(
                            Sb[sb][:, kb * 512:(kb + 1) * 512], B[bi][:, :]), "ac")
                        sc_banks[bi] = ("ac", v)
                    else:
                        ctx.wait("vector", "pe", pe_s)
                        if kb == 1 and i >= 2:
                            ctx.wait("vector", "ac", ex[i - 2])
                        v = ctx.op("vector", lambda e, kb=kb, bi=bi, sb=sb:
                                   e.tensor_copy(
                                       Sb[sb][:, kb * 512:(kb + 1) * 512],
                                       B[bi][:, :]), "ve")
                        sc_banks[bi] = ("ve", v)

            def emit_softmax(i):
                sb = i % 2
                ac_evac = ctx.cnt["ac"]
                ctx.wait("vector", "ac", ac_evac)
                if i >= 1:
                    ctx.wait("vector", "ac", ex[i - 1])
                mx = ctx.op("vector", lambda e, sb=sb: e.reduce_max(
                    negm, Sb[sb].rearrange("p (a s) -> p a s", s=8)[:, :, 0:1],
                    axis=AX.XY, negate=True), "ve")
                ctx.wait("scalar", "ve", mx)
                if i >= 2:
                    ctx.wait("scalar", "dt", dtv[i - 2])
                if i >= 1:
                    ctx.wait("scalar", "ve", iv[i - 1])
                ex[i] = ctx.op("scalar", lambda e, sb=sb: e.activation(
                    Pp[sb], Sb[sb], AF.Exp, bias=negm, accum_out=lcol), "ac")

            def emit_transpose(i):
                sb = i % 2
                ctx.wait("sync", "ac", ex[i])
                ctx.waitf("sync", pt_free[sb])
                dtv[i] = ctx.op("sync", lambda e, sb=sb: e.dma_start_transpose(
                    out=PTb[sb].rearrange("p (c q) -> p c q", c=NKC), in_=Pp[sb]),
                    "dt", 16)

            def emit_recip(i):
                sb = i % 2
                ctx.wait("vector", "ac", ex[i])
                iv[i] = ctx.op("vector", lambda e, sb=sb: e.reciprocal(
                    invl[:, sb:sb + 1], lcol), "ve")

            def emit_pv(i):
                nonlocal pv_bank, pt_free, pbf_state, vbuf_free
                h, qt = divmod(i, NQT)
                hp, hh = h // 2, h % 2
                sb = i % 2
                ctx.wait("tensor", "dt", dtv[i])
                ctx.waitf("tensor", pv_bank)
                ctx.waitf("tensor", vbuf_loaded)
                for kc in range(NKC):
                    ctx.op("tensor", lambda e, kc=kc, sb=sb: e.matmul(
                        B[3][:, 0:64], PTb[sb][:, kc * 128:(kc + 1) * 128],
                        vbuf[:, kc * 64:(kc + 1) * 64],
                        start=(kc == 0), stop=(kc == NKC - 1)), "pe")
                pe_pv = ctx.cnt["pe"]
                pt_free[sb] = ("pe", pe_pv)
                osl = i % 2
                ctx.wait("vector", "pe", pe_pv)
                ctx.waitf("vector", ot_slot[osl])
                ov = ctx.op("vector", lambda e, osl=osl, sb=sb: e.tensor_scalar(
                    sgt[:, osl * 64:osl * 64 + 64], B[3][:, 0:64],
                    invl[:, sb:sb + 1], RSQ, ALU.mult, ALU.mult), "ve")
                pv_bank = ("ve", ov)
                ctx.wait("tensor", "ve", ov)
                ctx.waitf("tensor", pbf_state)
                tv = ctx.op("tensor", lambda e, qt=qt, hh=hh, osl=osl: e.transpose(
                    PBF[hh * 64:(hh + 1) * 64, (qt % 8) * 128:(qt % 8 + 1) * 128],
                    sgt[:, osl * 64:osl * 64 + 64], eye), "pe")
                ot_slot[osl] = ("pe", tv)
                ctx.wait("vector", "pe", tv)
                cv = ctx.op("vector", lambda e, qt=qt, hh=hh, hp=hp: e.tensor_copy(
                    OT[hh * 64:(hh + 1) * 64,
                       hp * TOK + qt * 128:hp * TOK + (qt + 1) * 128],
                    PBF[hh * 64:(hh + 1) * 64, (qt % 8) * 128:(qt % 8 + 1) * 128]),
                    "ve")
                pbf_state = ("ve", cv)
                if qt == NQT - 1 and h + 1 < NH:
                    vbuf_free = [("pe", pe_pv)]
                    load_vbuf(h + 1)

            for i in range(NH * NQT + 2):
                if i < NH * NQT:
                    emit_scores(i)
                    emit_softmax(i)
                if i >= 2:
                    emit_pv(i - 2)
                if i < NH * NQT:
                    emit_transpose(i)
                    emit_recip(i)
            attn_pe_end = ctx.cnt["pe"]
            bo_b = [W[:, 4096:4608], W[:, 4608:5120]]
            xst = [W[:, 5120:5632], W[:, 5632:6144]]
            arena_fence = ("pe", attn_pe_end)
            bo_free = [[arena_fence], [arena_fence]]
            xst_free = [[arena_fence], [arena_fence]]
            ctx.wait("sync", "sp", ctx.cnt["sp"])
            hl3 = HLN(0, 4, 5, scratch=sut)
            for th in range(2):
                rlv = ctx.op("sync", lambda e, th=th: e.dma_start(
                    out=r_m, in_=rows2[th:th + 1, :]), "ws", 16)
                rlv = ctx.op("sync", lambda e, th=th: e.dma_start(
                    out=r_t, in_=rows2[2 + th:3 + th, :]), "ws", 16)
                ctx.waitf("tensor", bankstate[4])
                ctx.waitf("tensor", bankstate[5])
                ctx.wait("tensor", "ws", rlv)
                ctx.op("tensor", lambda e: e.matmul(
                    B[4][:, :], onesf, r_t, start=True, stop=True), "pe")
                ctx.op("tensor", lambda e: e.matmul(
                    B[5][:, :], onesf, r_m, start=True, stop=True), "pe")
                pe_bc = ctx.cnt["pe"]
                ctx.wait("scalar", "pe", pe_bc)
                ctx.waitf("scalar", bc_users)
                ctx.op("scalar", lambda e: e.copy(bc0, B[4][:, :]), "ac")
                ac_bc = ctx.op("scalar", lambda e: e.copy(bc1, B[5][:, :]), "ac")
                bankstate[4] = ("ac", ac_bc)
                bankstate[5] = ("ac", ac_bc)
                ctx.wait("vector", "ac", ac_bc)
                opjobs = []
                if th == 1:
                    opjobs = [(4, lambda: hl3.stats(16)),
                              (8, lambda: (hl3.stats(DC), hl3.rows_bcast()))]
                    opjobs += [(10 + 2 * i, lambda i=i: hl3.norm(4 * (i + 1)))
                               for i in range(8)]
                for jt in range(DC):
                    for (jj, fn) in opjobs:
                        if jj == jt:
                            fn()
                    sl = jt % 2
                    ctx.waitf("gpsimd", bo_free[sl])
                    cb = ctx.op("gpsimd", lambda e, jt=jt, sl=sl: e.dma_start(
                        out=bo_b[sl], in_=bow[jt]), "wg", 16)
                    ctx.waitf("sync", xst_free[sl])
                    xv = ctx.op("sync", lambda e, jt=jt, th=th, sl=sl: e.dma_start(
                        out=xst[sl],
                        in_=xspill[:, jt * TOK + th * GRP:jt * TOK + (th + 1) * GRP]),
                        "ws", 16)
                    ob_i = jt % 2
                    ctx.waitf("tensor", bankstate[ob_i])
                    ctx.wait("tensor", "wg", cb)
                    for hc in range(4):
                        ctx.op("tensor", lambda e, hc=hc, th=th, sl=sl, ob_i=ob_i:
                               e.matmul(
                                   B[ob_i][:, :], bo_b[sl][:, hc * 128:(hc + 1) * 128],
                                   OT[:, hc * TOK + th * GRP:hc * TOK + (th + 1) * GRP],
                                   start=(hc == 0), stop=(hc == 3)), "pe")
                    pe_o = ctx.cnt["pe"]
                    bo_free[sl] = [("pe", pe_o)]
                    eb = jt % 2
                    ctx.wait("vector", "pe", pe_o)
                    ctx.wait("vector", "ws", xv)
                    ctx.op("vector", lambda e, jt=jt, eb=eb, ob_i=ob_i:
                           e.scalar_tensor_tensor(
                               ett, B[ob_i][:, :],
                               obb[:, jt:jt + 1], bc1, ALU.add, ALU.add), "ve")
                    bankstate[ob_i] = ("ve", ctx.cnt["ve"])
                    ctx.op("vector", lambda e, sl=sl, eb=eb: e.tensor_mul(
                        sqt[:, eb * GRP:(eb + 1) * GRP], xst[sl], bc0), "ve")
                    xw = ctx.op("vector", lambda e, jt=jt, th=th, eb=eb: e.tensor_add(
                        Xg(jt, th), ett,
                        sqt[:, eb * GRP:(eb + 1) * GRP]), "ve")
                    xst_free[sl] = [("ve", xw)]
                bc_users = ("ve", ctx.cnt["ve"])
            wa = [("pe", ctx.cnt["pe"]), ("ve", ctx.cnt["ve"])]
            return ("ve", ctx.cnt["ve"]), wa, hl3

        if stop_after == "ffn1":
            xfence = ffn_pass(0, xready, None)
            dump_X(xfence)
        else:
            ln2a = HLN(0, 4, 5, keep_slot=0)
            ln2b = HLN(1, 2, 3, keep_slot=1, scratch=sut)

            def ln2_jobs(epi):
                jobs = [(15, lambda: ln2a.stats(16)),
                        (17, lambda: (ln2a.stats(DC), ln2a.rows_bcast()))]
                jobs += [(18 + i, lambda i=i: ln2a.norm(4 * (i + 1)))
                         for i in range(8)]
                for dc in range(DC - 3):
                    jobs.append((dc + 3, lambda dc=dc: ln2b.stats(
                        dc + 1, chunk_fence=("ve", epi[dc]))))
                jobs.append((DC, lambda: (
                    ln2b.stats(DC - 2, chunk_fence=("ve", epi[DC - 3])),
                    ln2b.stats(DC - 1, chunk_fence=("ve", epi[DC - 2])),
                    ln2b.stats(DC, chunk_fence=("ve", epi[DC - 1])),
                    ln2b.rows_bcast(), ln2b.norm(DC))))
                return jobs

            ffn_pass(0, xready, None, tail_jobs=ln2_jobs)
            af, wa, hl3 = attention(ln2a.fence + ln2b.fence)
            if stop_after == "attn":
                dump_X(af)
            else:
                ffn_pass(1, af, wa, g0_hosted=hl3)
        ctx.wait("sync", "outd", ctx.cnt["outd"])

        @block.sync
        def _(e):
            for fn in ctx.prog["sync"]:
                fn(e)

        @block.scalar
        def _(e):
            for fn in ctx.prog["scalar"]:
                fn(e)

        @block.vector
        def _(e):
            for fn in ctx.prog["vector"]:
                fn(e)

        @block.tensor
        def _(e):
            for fn in ctx.prog["tensor"]:
                fn(e)

        @block.gpsimd
        def _(e):
            for fn in ctx.prog["gpsimd"]:
                fn(e)

    return nc



def _prep(L0_attn_x, L0_attn_Wqk_w, L0_attn_Wqk_b, L0_attn_Wv_w, L0_attn_Wv_b,
          L0_attn_Wo_w, L0_attn_Wo_b, L0_ffn_gate_w, L0_ffn_gate_s, L0_ffn_up_w,
          L0_ffn_up_s, L0_ffn_down_w, L0_ffn_down_s):
    f32 = np.float32
    x = np.asarray(L0_attn_x, dtype=f32).reshape(SEQ, D)
    gate_w = np.asarray(L0_ffn_gate_w, dtype=np.int32)
    up_w = np.asarray(L0_ffn_up_w, dtype=np.int32)
    down_w = np.asarray(L0_ffn_down_w, dtype=np.int32)
    Wqk = np.asarray(L0_attn_Wqk_w, dtype=f32)
    Wv = np.asarray(L0_attn_Wv_w, dtype=f32)
    Wo = np.asarray(L0_attn_Wo_w, dtype=f32)

    def blk(Wm, nt):
        nch = Wm.shape[1] // 128
        return np.ascontiguousarray(
            Wm.reshape(nt, 128, nch, 128).transpose(0, 3, 2, 1)
            .reshape(nt, 128, Wm.shape[1]))

    g_b = blk(gate_w, FT)
    u_b = blk(up_w, FT)
    bgu = np.ascontiguousarray(np.concatenate(
        [g_b.reshape(FT, 128, 1, D), u_b.reshape(FT, 128, 1, D)], axis=2)
        .reshape(FT, 128, 2 * D))
    bd = blk(down_w, DC)
    P0 = []
    for hp in range(NHP):
        qrows = np.concatenate([(2 * hp) * 128 + 2 * np.arange(E),
                                (2 * hp + 1) * 128 + 2 * np.arange(E)])
        P0.append(qrows)
        P0.append(qrows + 1)
    P0 = np.concatenate(P0)
    bqk = blk(Wqk[P0], 8)
    bvw = blk(Wv, 4)
    bow = blk(Wo, DC)
    gs2 = np.ascontiguousarray(np.asarray(L0_ffn_gate_s, f32).reshape(FT, 128).T)
    us2 = np.ascontiguousarray(np.asarray(L0_ffn_up_s, f32).reshape(FT, 128).T)
    ds2 = np.ascontiguousarray(np.asarray(L0_ffn_down_s, f32).reshape(DC, 128).T)
    qkb2 = np.ascontiguousarray(np.asarray(L0_attn_Wqk_b, f32)[P0].reshape(8, 128).T)
    vb2 = np.ascontiguousarray(np.asarray(L0_attn_Wv_b, f32).reshape(4, 128).T)
    ob2 = np.ascontiguousarray(np.asarray(L0_attn_Wo_b, f32).reshape(DC, 128).T)
    eyeD = np.eye(128, dtype=BF)
    onesbD = np.ones((128, 1), dtype=BF)
    onesfD = np.ones((1, 128), dtype=np.float32)

    shared = dict(bgu=bgu, bd=bd, bqk=bqk, bvw=bvw, bow=bow, gs2=gs2, us2=us2,
                  ds2=ds2, qkb2=qkb2, vb2=vb2, ob2=ob2, eyeD=eyeD, onesbD=onesbD,
                  onesfD=onesfD)
    in_maps = []
    for c in range(NCORES):
        m = dict(shared)
        m["xT"] = np.ascontiguousarray(x[c * TOK:(c + 1) * TOK, :].T)
        in_maps.append(m)
    return in_maps


def kernel(**inputs):
    stop_after = inputs.pop("_stop_after", "full")
    in_maps = _prep(**inputs)
    nc = build(stop_after=stop_after)
    res = run_bass_kernel_spmd(nc, in_maps, core_ids=list(range(NCORES)))
    outs = [np.asarray(res.results[c]["out"]) for c in range(NCORES)]
    full = np.concatenate([o.T for o in outs], axis=0)
    return np.ascontiguousarray(full.reshape(1, SEQ, D), dtype=np.float32)
